# version 14
# baseline (speedup 1.0000x reference)
import numpy as np
import ml_dtypes
import concourse.bass as bass
import concourse.mybir as mybir
from concourse.bass_utils import run_bass_kernel_spmd
from contextlib import ExitStack

F32 = mybir.dt.float32
BF16 = mybir.dt.bfloat16
AF = mybir.ActivationFunctionType
ALU = mybir.AluOpType
AX = mybir.AxisListType

D = 2048
SEQ = 4096
OWN = 2048
NH = 16
DH = 128
FH = 8192
NDC = 16
NU = 64
SCALE_Q = 1.0 / float(np.sqrt(128.0))
S_SETS = ([0, 3, 4, 7], [1, 2, 5, 6])
KVS = [1024, 2048, 3072, 4096]
NEG = -1.0e30

ENGINES = ("tensor", "vector", "scalar", "gpsimd", "sync")


class Sem:
    __slots__ = ("h", "n", "name")

    def __init__(self, h, name):
        self.h = h
        self.n = 0
        self.name = name


class Rec:

    def __init__(self, nc, blk):
        self.nc = nc
        self.blk = blk
        self.ops = {e: [] for e in ENGINES}
        self.wm = {e: {} for e in ENGINES}

    def sem(self, name):
        return Sem(self.nc.alloc_semaphore(name), name)

    def op(self, eng, fn, waits=(), inc=None, inc_by=1):
        lst = self.ops[eng]
        for w in waits:
            if w is None:
                continue
            if isinstance(w, Sem):
                s, v = w, w.n
            else:
                s, v = w
            if v is None or v <= 0:
                continue
            if self.wm[eng].get(id(s), -1) < v:
                lst.append(("w", s, v))
                self.wm[eng][id(s)] = v
        if inc is not None:
            inc.n += inc_by
        if fn is not None:
            lst.append(("x", fn, inc, inc_by))
        return (inc, inc.n) if inc is not None else None

    def dma(self, q, out, in_, sem, waits=()):
        self.op(q, (lambda e, o=out, i=in_: e.dma_start(out=o, in_=i)),
                waits, inc=sem, inc_by=16)
        return (sem, sem.n)

    def dmaT(self, q, out, in_, sem, waits=()):
        self.op(q, (lambda e, o=out, i=in_: e.dma_start(out=o, in_=i,
                                                        transpose=True)),
                waits, inc=sem, inc_by=16)
        return (sem, sem.n)

    def fence(self, fences):
        for e in ENGINES:
            for (s, v) in fences:
                if v > 0 and self.wm[e].get(id(s), -1) < v:
                    self.ops[e].append(("w", s, v))
                    self.wm[e][id(s)] = v

    def flush(self):
        for e in ENGINES:
            lst = self.ops[e]
            if not lst:
                continue

            def body(eng, lst=tuple(lst)):
                for item in lst:
                    if item[0] == "w":
                        eng.wait_ge(item[1].h, item[2])
                    else:
                        _, fn, inc, k = item
                        ins = fn(eng)
                        if inc is not None:
                            ins.then_inc(inc.h, k)
            getattr(self.blk, e)(body)
            self.ops[e] = []


def _popcount_and(n):
    i = np.arange(n)
    pc = np.zeros((n, n), dtype=np.int64)
    a = i[:, None] & i[None, :]
    while np.any(a):
        pc += a & 1
        a >>= 1
    return pc


def _hadamard(n):
    return np.where(_popcount_and(n) % 2 == 0, 1.0, -1.0).astype(np.float32)


def _h64_i2():
    h64 = _hadamard(64)
    m = np.zeros((128, 128), dtype=np.float32)
    for r2 in range(2):
        m[r2::2, r2::2] = h64
    return m


def _amask(j):
    m = np.zeros((4, 4, 128, 1024), dtype=np.float32)
    for l in range(4):
        g = S_SETS[j][l]
        kvs = KVS[l]
        for i2 in range(4):
            qpos = g * 512 + i2 * 128 + np.arange(128)[:, None]
            jabs = kvs - 1024 + np.arange(1024)[None, :]
            m[l, i2] = np.where(jabs <= qpos, 0.0, NEG)
    return m.astype(ml_dtypes.bfloat16)


def _ln_tile(R, vS, aS, sX, sXsq, st, sXn, xld_wait, cEps, sq_guard=()):
    R.op("vector", lambda e, sX=sX, st=st: e.reduce_sum(st[:, 0:1], sX, axis=AX.X),
         waits=(xld_wait,), inc=vS)
    R.op("vector", lambda e, st=st: e.tensor_scalar_mul(st[:, 1:2], st[:, 0:1],
                                                        1.0 / D),
         waits=((vS, vS.n),), inc=vS)
    R.op("scalar", lambda e, sX=sX, st=st: e.activation(
        out=sXsq, in_=sX, func=AF.Square, accum_out=st[:, 2:3]),
        waits=(xld_wait,) + tuple(sq_guard), inc=aS)
    asq = (aS, aS.n)
    R.op("vector", lambda e, st=st: e.tensor_mul(st[:, 4:5], st[:, 1:2], st[:, 1:2]),
         waits=((vS, vS.n),), inc=vS)
    R.op("vector", lambda e, st=st: e.scalar_tensor_tensor(
        out=st[:, 3:4], in0=st[:, 2:3], scalar=1.0 / D, in1=st[:, 4:5],
        op0=ALU.mult, op1=ALU.subtract), waits=(asq, (vS, vS.n)), inc=vS)
    vvar = (vS, vS.n)
    R.op("scalar", lambda e, st=st, cEps=cEps: e.activation(
        out=st[:, 5:6], in_=st[:, 3:4], func=AF.Sqrt, bias=cEps[:, 0:1]),
        waits=(vvar,), inc=aS)
    asd = (aS, aS.n)
    R.op("vector", lambda e, st=st: e.reciprocal(st[:, 6:7], st[:, 5:6]),
         waits=(asd,), inc=vS)
    R.op("vector", lambda e, sX=sX, st=st, sXn=sXn: e.tensor_scalar(
        out=sXn, in0=sX, scalar1=st[:, 1:2], scalar2=st[:, 6:7],
        op0=ALU.subtract, op1=ALU.mult), waits=((vS, vS.n),), inc=vS)
    return (vS, vS.n)


def _xn_transposes(R, nc_t, sXn, sXnT, ts, pT, pT_free, tT, vS, cIdB,
                   xn_done, CONST, extra_guard, tkey):
    for k in range(4):
        pb = (tkey * 4 + k) % 2
        for q in range(4):
            dc = 4 * k + q
            w = (xn_done, CONST, pT_free[pb]) + tuple(extra_guard) if q == 0 else ()
            R.op("tensor",
                 lambda e, sXn=sXn, dc=dc, pb=pb, q=q: e.transpose(
                     pT[pb][:, 128 * q:128 * (q + 1)],
                     sXn[:, 128 * dc:128 * (dc + 1)], cIdB[:]),
                 waits=w, inc=tT if q == 3 else None)
        tdone = (tT, tT.n)
        R.op("vector",
             lambda e, k=k, ts=ts, pb=pb: e.tensor_copy(
                 out=sXnT[:, 4 * k:4 * (k + 1), 128 * ts:128 * (ts + 1)],
                 in_=pT[pb][:].rearrange("p (a b) -> p a b", a=4)),
             waits=(tdone,), inc=vS)
        pT_free[pb] = (vS, vS.n)


def build_program(debug=False):
    nc = bass.Bass("TRN2", target_bir_lowering=False, debug=False)

    xf = nc.dram_tensor("xf", [SEQ, D], F32, kind="ExternalInput").ap()
    xo = nc.dram_tensor("xo", [OWN, D], F32, kind="ExternalInput").ap()
    wq = nc.dram_tensor("wq", [D, D], F32, kind="ExternalInput").ap()
    wk = nc.dram_tensor("wk", [D, D], F32, kind="ExternalInput").ap()
    wv = nc.dram_tensor("wv", [D, D], F32, kind="ExternalInput").ap()
    wout = nc.dram_tensor("wout", [D, D], F32, kind="ExternalInput").ap()
    w1 = nc.dram_tensor("w1", [FH, D], F32, kind="ExternalInput").ap()
    w3 = nc.dram_tensor("w3", [FH, D], F32, kind="ExternalInput").ap()
    w2 = nc.dram_tensor("w2", [D, FH], F32, kind="ExternalInput").ap()
    amask = nc.dram_tensor("amask", [4, 4, 128, 1024], BF16, kind="ExternalInput").ap()
    h128 = nc.dram_tensor("h128", [128, 128], F32, kind="ExternalInput").ap()
    h64 = nc.dram_tensor("h64", [128, 128], F32, kind="ExternalInput").ap()
    identf = nc.dram_tensor("identf", [128, 128], F32, kind="ExternalInput").ap()
    out = nc.dram_tensor("out", [OWN, D], F32, kind="ExternalOutput").ap()

    WQB = nc.dram_tensor("WQB", [D, D], BF16, kind="Internal").ap()
    WKB = nc.dram_tensor("WKB", [D, D], BF16, kind="Internal").ap()
    WVB = nc.dram_tensor("WVB", [D, D], BF16, kind="Internal").ap()
    WOB = nc.dram_tensor("WOB", [D, D], BF16, kind="Internal").ap()
    KS = nc.dram_tensor("KS", [D, SEQ], BF16, kind="Internal").ap()
    VS = nc.dram_tensor("VS", [NH, 128, 32, DH], BF16, kind="Internal").ap()
    QS = nc.dram_tensor("QS", [4, 128, NH, 512], BF16, kind="Internal").ap()
    XNT = nc.dram_tensor("XNT", [4, 128, NDC, 512], BF16, kind="Internal").ap()
    AS = nc.dram_tensor("AS", [NH, 128, OWN], BF16, kind="Internal").ap()
    W13S = nc.dram_tensor("W13S", [NU, 128, 2 * D], BF16, kind="Internal").ap()
    W2S = nc.dram_tensor("W2S", [NU, 128, D], BF16, kind="Internal").ap()

    dbg = {}
    if debug:
        for name, t, shape in (("dKS", KS, [D, SEQ]), ("dVS", VS, [NH, 128, 32, DH]),
                               ("dQS", QS, [4, 128, NH, 512]),
                               ("dXNT", XNT, [4, 128, NDC, 512]),
                               ("dAS", AS, [NH, 128, OWN]),
                               ("dW13S", W13S, [NU, 128, 2 * D]),
                               ("dW2S", W2S, [NU, 128, D])):
            dbg[name] = (nc.dram_tensor(name, shape, BF16, kind="ExternalOutput").ap(), t)

    with ExitStack() as es_all:
        cIdB = es_all.enter_context(nc.sbuf_tensor("cIdB", [128, 128], BF16)).ap()
        cIdF = es_all.enter_context(nc.sbuf_tensor("cIdF", [128, 128], F32)).ap()
        cH128 = es_all.enter_context(nc.sbuf_tensor("cH128", [128, 128], BF16)).ap()
        cH64 = es_all.enter_context(nc.sbuf_tensor("cH64", [128, 128], BF16)).ap()
        cOnesF = es_all.enter_context(nc.sbuf_tensor("cOnesF", [1, 128], F32)).ap()
        cEps = es_all.enter_context(nc.sbuf_tensor("cEps", [128, 1], F32)).ap()

        blk = es_all.enter_context(nc.Block())
        R = Rec(nc, blk)

        cst = R.sem("cst")
        R.dma("gpsimd", cIdB[:], identf[:], cst)
        R.dma("sync", cIdF[:], identf[:], cst)
        R.dma("gpsimd", cH128[:], h128[:], cst)
        R.dma("gpsimd", cH64[:], h64[:], cst)
        CONST = (cst, cst.n)
        vcst = R.sem("vcst")
        R.op("vector", lambda e: e.memset(cOnesF[:], 1.0), inc=vcst)
        R.op("vector", lambda e: e.memset(cEps[:], 1.0e-5), inc=vcst)
        ONES = (vcst, vcst.n)

        WKR = R.dma("gpsimd", WKB[:, :], wk[:, :], R.sem("cwk"))
        WVR = R.dma("gpsimd", WVB[:, :], wv[:, :], R.sem("cwv"))
        WQR = R.dma("gpsimd", WQB[:, :], wq[:, :], R.sem("cwq"))
        WOR = R.dma("gpsimd", WOB[:, :], wout[:, :], R.sem("cwo"))

        H = dict(xf=xf, xo=xo, wq=WQB, wk=WKB, wv=WVB, wout=WOB, w1=w1,
                 w3=w3, w2=w2, amask=amask, out=out, KS=KS, VS=VS, QS=QS,
                 XNT=XNT, AS=AS, W13S=W13S, W2S=W2S, cIdB=cIdB, cIdF=cIdF,
                 cH128=cH128, cH64=cH64, cOnesF=cOnesF, cEps=cEps, CONST=CONST,
                 ONES=ONES, WKR=WKR, WVR=WVR, WQR=WQR, WOR=WOR)

        R.fence([CONST, ONES])
        f1 = _phase1(nc, R, H)
        R.fence(f1)
        f2 = _phase2(nc, R, H)
        R.fence(f2)
        f3 = _phase3(nc, R, H)
        R.fence(f3)
        oSg = _phase4(nc, R, H)

        if debug:
            dsem = R.sem("dbgs")
            for name, (o, t) in dbg.items():
                R.dma("sync", o[:], t[:], dsem)
            R.op("sync", None, waits=((dsem, dsem.n),))

        R.op("sync", None, waits=tuple((s, s.n) for s in oSg))
        R.flush()

    return nc


def _phase1(nc, R, H):
    with ExitStack() as es:
        sWk = es.enter_context(nc.sbuf_tensor("sWk", [128, NDC, D], BF16)).ap()
        sWv = es.enter_context(nc.sbuf_tensor("sWv", [128, NDC, D], BF16)).ap()
        sX = es.enter_context(nc.sbuf_tensor("sX", [128, 2, D], F32)).ap()
        sStat = es.enter_context(nc.sbuf_tensor("sStat", [128, 2, 8], F32)).ap()
        sXn = es.enter_context(nc.sbuf_tensor("sXn", [128, 2, D], BF16)).ap()
        sXnT = es.enter_context(nc.sbuf_tensor("sXnT", [128, NDC, 512], BF16)).ap()
        sKev = es.enter_context(nc.sbuf_tensor("sKev", [128, 2, 512], BF16)).ap()
        sVst = es.enter_context(nc.sbuf_tensor("sVst", [128, NH, 4, DH], BF16)).ap()
        sWld = es.enter_context(nc.sbuf_tensor("sWld", [128, 2, D], F32)).ap()
        sWst = es.enter_context(nc.sbuf_tensor("sWst", [128, 1, NDC * 128], BF16)).ap()
        pT = [es.enter_context(nc.psum_tensor(f"pT{i}", [128, 512], BF16)).ap()
              for i in range(2)]
        pK = [es.enter_context(nc.psum_tensor(f"pK{i}", [128, 512], F32)).ap()
              for i in range(2)]
        pV = [es.enter_context(nc.psum_tensor(f"pV{i}", [128, 512], F32)).ap()
              for i in range(2)]
        pWT = [es.enter_context(nc.psum_tensor(f"pWT{i}", [128, 512], F32)).ap()
               for i in range(2)]

        xldg = [R.sem("p1_xld0"), R.sem("p1_xld1")]
        wkvS = R.sem("p1_wkv")
        wldg = [R.sem("p1_wld0"), R.sem("p1_wld1")]
        vS = R.sem("p1_v")
        aS = R.sem("p1_a")
        tT = R.sem("p1_tT")
        tK = R.sem("p1_tK")
        tV = R.sem("p1_tV")
        tW = R.sem("p1_tW")
        kSg = [R.sem("p1_kS0"), R.sem("p1_kS1")]
        vsS = R.sem("p1_vsS")
        wSg = [R.sem("p1_wS0"), R.sem("p1_wS1")]

        xf, wk, wv, w1, w3 = H["xf"], H["wk"], H["wv"], H["w1"], H["w3"]
        KS, VS, W13S = H["KS"], H["VS"], H["W13S"]
        cIdB, CONST = H["cIdB"], H["CONST"]

        for w_src, w_dst, w_rdy in ((wk, sWk, H["WKR"]), (wv, sWv, H["WVR"])):
            for q in range(4):
                R.dma("scalar",
                      w_dst[:, 4 * q:4 * (q + 1), :],
                      w_src.rearrange("(c p) n -> p c n", p=128)[:, 4 * q:4 * (q + 1), :],
                      wkvS, waits=(w_rdy,))
        WKV = (wkvS, wkvS.n)

        sX_free = [None, None]
        pT_free = [None, None]
        pK_free = [None, None]
        pV_free = [None, None]
        sKev_free = [None, None]
        sWld_free = [None, None]
        sWst_free = [None, None]
        pWT_free = [None, None]
        sVst_free = [None]
        sXn_guard = [None, None]
        state = {"n_prep": 0}

        def prep_unit():
            pi = state["n_prep"]
            state["n_prep"] += 1
            if pi >= 128:
                return
            mat_src = w1 if pi < 64 else w3
            off = 0 if pi < 64 else D
            u = pi % 64
            b = pi % 2
            R.dma("scalar", sWld[:, b, :],
                  mat_src.rearrange("(a rr) d -> a rr d", rr=128)[:, 2 * u:2 * u + 2, :],
                  wldg[b], waits=(sWld_free[b],))
            ld = (wldg[b], wldg[b].n)
            for k in range(4):
                pb = (pi * 4 + k) % 2
                for q in range(4):
                    dc = 4 * k + q
                    R.op("tensor",
                         lambda e, b=b, dc=dc, pb=pb, q=q: e.transpose(
                             pWT[pb][:, 128 * q:128 * (q + 1)],
                             sWld[:, b, 128 * dc:128 * (dc + 1)], H["cIdF"][:]),
                         waits=(ld, CONST, pWT_free[pb]) if q == 0 else (),
                         inc=tW if q == 3 else None)
                tdone = (tW, tW.n)
                R.op("vector",
                     lambda e, k=k, pb=pb: e.tensor_copy(
                         out=sWst[:, 0, 512 * k:512 * (k + 1)], in_=pWT[pb][:]),
                     waits=(tdone, sWst_free[0]), inc=vS)
                pWT_free[pb] = (vS, vS.n)
            evdone = (vS, vS.n)
            sWld_free[b] = evdone
            R.dma("sync", W13S[u, :, off:off + D], sWst[:, 0, :], wSg[0],
                  waits=(evdone,))
            sWst_free[0] = wSg[0]

        for g in range(8):
            for ts in range(4):
                t = 4 * g + ts
                b = t % 2
                R.dma("sync", sX[:, b, :], xf[128 * t:128 * (t + 1), :],
                      xldg[b], waits=(sX_free[b],))
                xw = (xldg[b], xldg[b].n)
                xn_done = _ln_tile(R, vS, aS, sX[:, b, :], sXn[:, b, :],
                                   sStat[:, b, :], sXn[:, b, :], xw, H["cEps"],
                                   sq_guard=(sXn_guard[b],))
                sX_free[b] = xn_done
                _xn_transposes(R, nc, sXn[:, b, :], sXnT, ts, pT, pT_free,
                               tT, vS, cIdB, xn_done, CONST, (), t)
                sXn_guard[b] = (tT, tT.n)
                prep_unit()
                prep_unit()
            xnt_ready = (vS, vS.n)

            for kc in range(NDC):
                pb = kc % 2
                for dc in range(NDC):
                    R.op("tensor",
                         lambda e, kc=kc, dc=dc, pb=pb: e.matmul(
                             pK[pb][:], sWk[:, dc, 128 * kc:128 * (kc + 1)],
                             sXnT[:, dc, :], start=(dc == 0), stop=(dc == NDC - 1)),
                         waits=(xnt_ready, WKV, pK_free[pb]) if dc == 0 else (),
                         inc=tK if dc == NDC - 1 else None)
                mmdone = (tK, tK.n)
                eb = kc % 2
                R.op("scalar",
                     lambda e, eb=eb, pb=pb: e.activation(
                         out=sKev[:, eb, :], in_=pK[pb][:], func=AF.Copy),
                     waits=(mmdone, sKev_free[eb]), inc=aS)
                pK_free[pb] = (aS, aS.n)
                R.dma("sync", KS[128 * kc:128 * (kc + 1), 512 * g:512 * (g + 1)],
                      sKev[:, eb, :], kSg[eb], waits=((aS, aS.n),))
                sKev_free[eb] = kSg[eb]

            for ts in range(4):
                for vc in range(4):
                    pb = (ts * 4 + vc) % 2
                    for dc in range(NDC):
                        R.op("tensor",
                             lambda e, ts=ts, vc=vc, dc=dc, pb=pb: e.matmul(
                                 pV[pb][:], sXnT[:, dc, 128 * ts:128 * (ts + 1)],
                                 sWv[:, dc, 512 * vc:512 * (vc + 1)],
                                 start=(dc == 0), stop=(dc == NDC - 1)),
                             waits=(xnt_ready, WKV, pV_free[pb], sVst_free[0])
                             if dc == 0 else (),
                             inc=tV if dc == NDC - 1 else None)
                    mmdone = (tV, tV.n)
                    for hq in range(4):
                        h = 4 * vc + hq
                        R.op("vector",
                             lambda e, h=h, ts=ts, hq=hq, pb=pb: e.tensor_copy(
                                 out=sVst[:, h, ts, :],
                                 in_=pV[pb][:, 128 * hq:128 * (hq + 1)]),
                             waits=(mmdone,), inc=vS)
                    pV_free[pb] = (vS, vS.n)
            vst_done = (vS, vS.n)
            for h in range(NH):
                R.dma("sync", VS[h, :, 4 * g:4 * (g + 1), :], sVst[:, h, :, :],
                      vsS, waits=(vst_done,))
            sVst_free[0] = vsS

            prep_unit()
            prep_unit()
            prep_unit()
            prep_unit()

        while state["n_prep"] < 128:
            prep_unit()

        R.flush()
        sems = xldg + [wkvS] + wldg + [vS, aS, tT, tK, tV, tW] + kSg + \
            [vsS] + wSg
        return [(s, s.n) for s in sems]


def _phase2(nc, R, H):
    with ExitStack() as es:
        sWq = es.enter_context(nc.sbuf_tensor("sWq", [128, NDC, D], BF16)).ap()
        sX = es.enter_context(nc.sbuf_tensor("p2sX", [128, 2, D], F32)).ap()
        sXsq = es.enter_context(nc.sbuf_tensor("p2sXsq", [128, D], BF16)).ap()
        sStat = es.enter_context(nc.sbuf_tensor("p2sStat", [128, 2, 8], F32)).ap()
        sXn = es.enter_context(nc.sbuf_tensor("p2sXn", [128, 2, D], BF16)).ap()
        sXnT = es.enter_context(nc.sbuf_tensor("p2sXnT", [128, NDC, 512], BF16)).ap()
        sQev = es.enter_context(nc.sbuf_tensor("sQev", [128, 2, 512], BF16)).ap()
        sW2ld = es.enter_context(nc.sbuf_tensor("sW2ld", [128, 2, FH // 4], F32)).ap()
        sStg2 = es.enter_context(nc.sbuf_tensor("sStg2", [128, NU, 512], BF16)).ap()
        sW2ev = es.enter_context(nc.sbuf_tensor("sW2ev", [128, 2, 512], BF16)).ap()
        pT = [es.enter_context(nc.psum_tensor(f"p2pT{i}", [128, 512], BF16)).ap()
              for i in range(2)]
        pQ = [es.enter_context(nc.psum_tensor(f"p2pQ{i}", [128, 512], F32)).ap()
              for i in range(2)]
        pWT = [es.enter_context(nc.psum_tensor(f"p2pWT{i}", [128, 512], F32)).ap()
               for i in range(2)]
        pA = [es.enter_context(nc.psum_tensor(f"p2pA{i}", [128, 512], F32)).ap()
              for i in range(2)]

        xldg = [R.sem("p2_xld0"), R.sem("p2_xld1")]
        wqS = R.sem("p2_wq")
        w2ldg = [R.sem("p2_w2l0"), R.sem("p2_w2l1")]
        vS = R.sem("p2_v")
        aS = R.sem("p2_a")
        tT = R.sem("p2_tT")
        tQ = R.sem("p2_tQ")
        tW = R.sem("p2_tW")
        tA = R.sem("p2_tA")
        qSg = [R.sem("p2_qS0"), R.sem("p2_qS1")]
        xS = R.sem("p2_xS")
        w2Sg = [R.sem("p2_w2S0"), R.sem("p2_w2S1")]

        xo, wq, w2 = H["xo"], H["wq"], H["w2"]
        QS, XNT, W2S = H["QS"], H["XNT"], H["W2S"]
        cIdB, cH128, CONST = H["cIdB"], H["cH128"], H["CONST"]

        for q in range(4):
            R.dma("scalar", sWq[:, 4 * q:4 * (q + 1), :],
                  wq.rearrange("(c p) n -> p c n", p=128)[:, 4 * q:4 * (q + 1), :],
                  wqS, waits=(H["WQR"],))
        WQ = (wqS, wqS.n)

        sX_free = [None, None]
        pT_free = [None, None]
        pQ_free = [None, None]
        sQev_free = [None, None]
        pWT_free = [None, None]
        pA_free = [None, None]
        sW2ev_free = [None, None]
        sW2ld_free = [None, None]
        sXn_guard = [None, None]
        sXnT_guard = []
        sStg2_guard = []

        for g in range(4):
            for ts in range(4):
                t = 4 * g + ts
                b = t % 2
                R.dma("sync", sX[:, b, :], xo[128 * t:128 * (t + 1), :],
                      xldg[b], waits=(sX_free[b],))
                xw = (xldg[b], xldg[b].n)
                xn_done = _ln_tile(R, vS, aS, sX[:, b, :], sXn[:, b, :],
                                   sStat[:, b, :], sXn[:, b, :], xw, H["cEps"],
                                   sq_guard=(sXn_guard[b],))
                sX_free[b] = xn_done
                _xn_transposes(R, nc, sXn[:, b, :], sXnT, ts, pT, pT_free,
                               tT, vS, cIdB, xn_done, CONST, sXnT_guard, t)
                sXn_guard[b] = (tT, tT.n)
            xnt_ready = (vS, vS.n)
            R.dma("sync", XNT[g, :, :, :], sXnT[:, :, :], xS, waits=(xnt_ready,))
            xnt_stashed = (xS, xS.n)

            for qc in range(NDC):
                pb = qc % 2
                for dc in range(NDC):
                    R.op("tensor",
                         lambda e, qc=qc, dc=dc, pb=pb: e.matmul(
                             pQ[pb][:], sWq[:, dc, 128 * qc:128 * (qc + 1)],
                             sXnT[:, dc, :], start=(dc == 0), stop=(dc == NDC - 1)),
                         waits=(xnt_ready, WQ, pQ_free[pb]) if dc == 0 else (),
                         inc=tQ if dc == NDC - 1 else None)
                mmdone = (tQ, tQ.n)
                eb = qc % 2
                R.op("scalar",
                     lambda e, eb=eb, pb=pb: e.activation(
                         out=sQev[:, eb, :], in_=pQ[pb][:], func=AF.Copy,
                         scale=SCALE_Q),
                     waits=(mmdone, sQev_free[eb]), inc=aS)
                pQ_free[pb] = (aS, aS.n)
                R.dma("sync", QS[g, :, qc, :], sQev[:, eb, :], qSg[eb],
                      waits=((aS, aS.n),))
                sQev_free[eb] = qSg[eb]
            sXnT_guard = [(tQ, tQ.n), xS]

            for dq in range(4):
                dc = 4 * g + dq
                for hh in range(4):
                    lb = (dc * 4 + hh) % 2
                    R.dma("scalar", sW2ld[:, lb, :],
                          w2[128 * dc:128 * (dc + 1),
                             2048 * hh:2048 * (hh + 1)],
                          w2ldg[lb], waits=(sW2ld_free[lb],))
                    ld = (w2ldg[lb], w2ldg[lb].n)
                    for k in range(4):
                        ka = 4 * hh + k
                        pb = (dc * 16 + ka) % 2
                        for q in range(4):
                            a = 4 * k + q
                            w = (ld, CONST, pWT_free[pb]) + tuple(sStg2_guard) \
                                if q == 0 else ()
                            R.op("tensor",
                                 lambda e, a=a, pb=pb, q=q, lb=lb: e.transpose(
                                     pWT[pb][:, 128 * q:128 * (q + 1)],
                                     sW2ld[:, lb, 128 * a:128 * (a + 1)],
                                     H["cIdF"][:]),
                                 waits=w, inc=tW if q == 3 else None)
                        tdone = (tW, tW.n)
                        R.op("vector",
                             lambda e, ka=ka, dq=dq, pb=pb: e.tensor_copy(
                                 out=sStg2[:, 4 * ka:4 * (ka + 1),
                                           128 * dq:128 * (dq + 1)],
                                 in_=pWT[pb][:].rearrange("p (a b) -> p a b", a=4)),
                             waits=(tdone,), inc=vS)
                        pWT_free[pb] = (vS, vS.n)
                    sW2ld_free[lb] = (tW, tW.n)
            stg_ready = (vS, vS.n)
            for a in range(NU):
                pb = a % 2
                R.op("tensor",
                     lambda e, a=a, pb=pb: e.matmul(
                         pA[pb][:], cH128[:], sStg2[:, a, :],
                         start=True, stop=True),
                     waits=(stg_ready, CONST, pA_free[pb]), inc=tA)
                mmdone = (tA, tA.n)
                eb = a % 2
                R.op("scalar",
                     lambda e, eb=eb, pb=pb: e.activation(
                         out=sW2ev[:, eb, :], in_=pA[pb][:], func=AF.Copy,
                         scale=SCALE_Q),
                     waits=(mmdone, sW2ev_free[eb]), inc=aS)
                pA_free[pb] = (aS, aS.n)
                R.dma("sync", W2S[:, 2 * a:2 * a + 2, 512 * g:512 * (g + 1)],
                      sW2ev[:, eb, :], w2Sg[eb], waits=((aS, aS.n),))
                sW2ev_free[eb] = w2Sg[eb]
            sStg2_guard = [(tA, tA.n)]

        R.flush()
        sems = xldg + [wqS] + w2ldg + [vS, aS, tT, tQ, tW, tA, xS] + qSg + w2Sg
        return [(s, s.n) for s in sems]


def _phase3(nc, R, H):
    with ExitStack() as es:
        sAmL = es.enter_context(nc.sbuf_tensor("sAmL", [128, 2, 4, 1024], BF16)).ap()
        sS = es.enter_context(nc.sbuf_tensor("sS", [128, 4, 4096], F32)).ap()
        sP = es.enter_context(nc.sbuf_tensor("sP", [128, 2, 4, 4096], BF16)).ap()
        sPT = es.enter_context(nc.sbuf_tensor("sPT", [128, 4, 512], BF16)).ap()
        sK = es.enter_context(nc.sbuf_tensor("sK", [128, 2, 4096], BF16)).ap()
        sV = es.enter_context(nc.sbuf_tensor("sV", [128, 2, 32, DH], BF16)).ap()
        sQc = es.enter_context(nc.sbuf_tensor("sQc", [128, 2, 512], BF16)).ap()
        sCM = es.enter_context(nc.sbuf_tensor("sCM", [128, 4, 4], F32)).ap()
        sNM = es.enter_context(nc.sbuf_tensor("sNM", [128, 2, 4], F32)).ap()
        sL = es.enter_context(nc.sbuf_tensor("sL", [128, 2, 4], F32)).ap()
        sRi = es.enter_context(nc.sbuf_tensor("sRi", [128, 2, 4], F32)).ap()
        sA = es.enter_context(nc.sbuf_tensor("sA", [128, 2, 512], BF16)).ap()
        pS = [es.enter_context(nc.psum_tensor(f"p3pS{i}", [128, 2, 512], F32)).ap()
              for i in range(2)]
        pTT = es.enter_context(nc.psum_tensor("p3pTT", [128, 2, 512], BF16)).ap()
        pO = [es.enter_context(nc.psum_tensor(f"p3pO{i}", [128, 512], F32)).ap()
              for i in range(2)]

        dLg = [R.sem("p3_dL0"), R.sem("p3_dL1")]
        mldg = [R.sem("p3_mld0"), R.sem("p3_mld1")]
        dAg = [R.sem("p3_dA0"), R.sem("p3_dA1")]
        tQK = R.sem("p3_tQK")
        tTP = R.sem("p3_tTP")
        tPV = R.sem("p3_tPV")
        aSE = R.sem("p3_aSE")
        aPT = R.sem("p3_aPT")
        aE = R.sem("p3_aE")
        aA = R.sem("p3_aA")
        vSE = R.sem("p3_vSE")
        vSum = R.sem("p3_vSum")
        gP = R.sem("p3_gP")
        gM = R.sem("p3_gM")

        KS, VS, QS, AS, amask = H["KS"], H["VS"], H["QS"], H["AS"], H["amask"]
        CONST = H["CONST"]

        ld_free = [None, None]
        pS_free = [None, None]
        pTT_free = [None]
        pO_free = [None, None]
        sA_free = [None]
        sPT_base = [0]
        sNM_guard = [None, None]
        sL_guard = [None, None]
        sRi_guard = [None, None]
        amask_guard = [[], []]
        state = {}
        units = [(l, h) for l in range(4) for h in range(NH)]

        def deferred(p):
            l, h, kv, nb, n5, par, _exp_done, scale_done, _tTf = state[p]
            ev_jb = {}
            pv_base = [None]

            def emit_Tpair(jp):
                for sub in range(2):
                    jb = 2 * jp + sub
                    for i2 in range(4):
                        R.op("tensor",
                             lambda e, jb=jb, i2=i2, sub=sub, par=par: e.transpose(
                                 pTT[:, sub, 128 * i2:128 * (i2 + 1)],
                                 sP[:, par, i2, 128 * jb:128 * (jb + 1)],
                                 H["cIdB"][:]),
                             waits=(scale_done, CONST, pTT_free[0])
                             if (sub == 0 and i2 == 0) else (),
                             inc=tTP if (sub == 1 and i2 == 3) else None)
                tdone = (tTP, tTP.n)
                slot = (2 * jp) % 4
                w = [tdone]
                if pv_base[0] is None:
                    pv_base[0] = tPV.n
                if jp >= 2:
                    w.append((tPV, pv_base[0] + 2 * jp - 2))
                if jp % 2 == 0:
                    R.op("vector",
                         lambda e, jp=jp, slot=slot: e.tensor_copy(
                             out=sPT[:, slot:slot + 2, :], in_=pTT[:, :, :]),
                         waits=tuple(w), inc=vSE)
                    ev_jb[jp] = (vSE, vSE.n)
                else:
                    R.op("scalar",
                         lambda e, jp=jp, slot=slot: e.activation(
                             out=sPT[:, slot:slot + 2, :], in_=pTT[:, :, :],
                             func=AF.Identity),
                         waits=tuple(w), inc=aPT)
                    ev_jb[jp] = (aPT, aPT.n)
                pTT_free[0] = ev_jb[jp]

            def emit_PV(jb):
                w = [ev_jb[jb // 2]]
                if jb == 0:
                    w.append(pO_free[par])
                R.op("tensor",
                     lambda e, par=par, jb=jb, nb=nb: e.matmul(
                         pO[par][:], sV[:, par, jb, :], sPT[:, jb % 4, :],
                         start=(jb == 0), stop=(jb == nb - 1)),
                     waits=tuple(w), inc=tPV)

            steps = [lambda: emit_Tpair(0)]
            for jp in range(1, nb // 2):
                steps.append(lambda jp=jp: (emit_Tpair(jp), emit_PV(2 * jp - 2),
                                            emit_PV(2 * jp - 1)))
            steps.append(lambda: (emit_PV(nb - 2), emit_PV(nb - 1)))

            def tail():
                pv_done = (tPV, tPV.n)
                ld_free[par] = pv_done
                R.op("scalar",
                     lambda e, par=par: e.activation(
                         out=sA[:, 0, :], in_=pO[par][:], func=AF.Identity),
                     waits=(pv_done, sA_free[0]), inc=aA)
                pO_free[par] = (aA, aA.n)
                R.dma("scalar", AS[h, :, 512 * l:512 * (l + 1)], sA[:, 0, :],
                      dAg[par], waits=((aA, aA.n),))
                sA_free[0] = dAg[par]
                if h == NH - 1:
                    amask_guard[l % 2] = [(gM, gM.n)]
            return steps, tail

        for idx, (l, h) in enumerate(units):
            kv = KVS[l]
            nb = kv // 128
            n5 = kv // 512
            par = idx % 2
            ms = l % 2
            if h == 0:
                R.op("sync", None, waits=tuple(amask_guard[ms]))
                R.dma("sync", sAmL[:, ms, :, :],
                      amask.rearrange("l i p w -> l p i w")[l], mldg[ms])
            mask_ready = (mldg[ms], mldg[ms].n)
            R.dma("sync", sK[:, par, 0:kv], KS[128 * h:128 * (h + 1), 0:kv],
                  dLg[par], waits=(ld_free[par],))
            R.dma("sync", sV[:, par, 0:nb, :], VS[h, :, 0:nb, :], dLg[par])
            R.dma("sync", sQc[:, par, :], QS[l, :, h, :], dLg[par])
            lds = (dLg[par], dLg[par].n)

            prev_exp = state[idx - 1][6] if idx > 0 else None
            pexp_guard = state[idx - 1][8] if idx > 1 else None
            first_evac = [True]
            bcnt = [0]
            nbat = n5 // 2

            def emit_qk_batch(i2, bi, idx=idx, l=l, kv=kv, n5=n5, nbat=nbat,
                              par=par, ms=ms, lds=lds, mask_ready=mask_ready,
                              prev_exp=prev_exp, pexp_guard=pexp_guard):
                pp = bcnt[0] % 2
                for k in range(2):
                    j5 = 2 * bi + k
                    R.op("tensor",
                         lambda e, par=par, i2=i2, j5=j5, k=k, pp=pp: e.matmul(
                             pS[pp][:, k, :],
                             sQc[:, par, 128 * i2:128 * (i2 + 1)],
                             sK[:, par, 512 * j5:512 * (j5 + 1)],
                             start=True, stop=True),
                         waits=(lds, pS_free[pp]) if k == 0 else (),
                         inc=tQK if k == 1 else None)
                mmdone = (tQK, tQK.n)
                j0 = 2 * bi
                w = [mmdone]
                if first_evac[0]:
                    w.append(prev_exp)
                    first_evac[0] = False
                if bcnt[0] % 4 == 3:
                    R.op("vector",
                         lambda e, i2=i2, j0=j0, pp=pp: e.tensor_copy(
                             out=sS[:, i2, 512 * j0:512 * (j0 + 2)],
                             in_=pS[pp][:, :, :]),
                         waits=tuple(w), inc=vSE)
                    ev = (vSE, vSE.n)
                else:
                    R.op("scalar",
                         lambda e, i2=i2, j0=j0, pp=pp: e.activation(
                             out=sS[:, i2, 512 * j0:512 * (j0 + 2)],
                             in_=pS[pp][:, :, :], func=AF.Identity),
                         waits=tuple(w), inc=aSE)
                    ev = (aSE, aSE.n)
                pS_free[pp] = ev
                if bi == nbat - 1:
                    R.op("gpsimd",
                         lambda e, i2=i2, kv=kv, ms=ms: e.tensor_tensor(
                             out=sS[:, i2, kv - 1024:kv],
                             in0=sS[:, i2, kv - 1024:kv],
                             in1=sAmL[:, ms, i2, :], op=ALU.add),
                         waits=(ev, mask_ready), inc=gM)
                    ev = (gM, gM.n)
                R.op("vector",
                     lambda e, i2=i2, bi=bi, j0=j0: e.reduce_max(
                         sCM[:, i2, bi:bi + 1],
                         sS[:, i2, 512 * j0:512 * (j0 + 2)], axis=AX.X),
                     waits=(ev,), inc=vSE)
                bcnt[0] += 1
                if bi == nbat - 1:
                    R.op("vector",
                         lambda e, i2=i2, nbat=nbat, par=par: e.reduce_max(
                             sNM[:, par, i2:i2 + 1], sCM[:, i2, 0:nbat],
                             axis=AX.X, negate=True),
                         waits=((vSE, vSE.n), sNM_guard[par]), inc=vSE)
                    nm = (vSE, vSE.n)
                    R.op("scalar",
                         lambda e, i2=i2, kv=kv, par=par: e.activation(
                             out=sP[:, par, i2, 0:kv], in_=sS[:, i2, 0:kv],
                             func=AF.Exp, bias=sNM[:, par, i2:i2 + 1],
                             accum_out=sL[:, par, i2:i2 + 1]),
                         waits=(nm, sL_guard[par], pexp_guard), inc=aE)
                    R.op("vector",
                         lambda e, i2=i2, par=par: e.reciprocal(
                             sRi[:, par, i2:i2 + 1], sL[:, par, i2:i2 + 1]),
                         waits=((aE, aE.n), sRi_guard[par]), inc=vSum)
                    R.op("vector",
                         lambda e, i2=i2, kv=kv, par=par: e.tensor_scalar_mul(
                             sP[:, par, i2, 0:kv], sP[:, par, i2, 0:kv],
                             sRi[:, par, i2:i2 + 1]),
                         waits=((vSum, vSum.n),), inc=gP)

            qk_steps = [(i2, bi) for i2 in range(4) for bi in range(nbat)]
            if idx > 0:
                pair_steps, def_tail = deferred(idx - 1)
            else:
                pair_steps, def_tail = [], None
            nq, nd = len(qk_steps), len(pair_steps)
            qi = di = 0
            while qi < nq or di < nd:
                take_q = 2 if qi == 0 else 1
                for _ in range(take_q):
                    if qi < nq:
                        emit_qk_batch(*qk_steps[qi])
                        qi += 1
                if nq - qi <= 0:
                    while di < nd:
                        pair_steps[di]()
                        di += 1
                else:
                    want = int(round((qi / nq) * nd))
                    while di < min(want, nd):
                        pair_steps[di]()
                        di += 1
            if def_tail is not None:
                def_tail()

            sNM_guard[par] = (aE, aE.n)
            sL_guard[par] = (vSum, vSum.n)
            sRi_guard[par] = (gP, gP.n)
            state[idx] = (l, h, kv, nb, n5, par, (aE, aE.n), (gP, gP.n),
                          (tTP, tTP.n))
        last_steps, last_tail = deferred(len(units) - 1)
        for st_fn in last_steps:
            st_fn()
        last_tail()

        R.flush()
        sems = dLg + dAg + mldg + [tQK, tTP, tPV, aSE, aPT, aE, aA, vSE,
                                   vSum, gP, gM]
        return [(s, s.n) for s in sems]


def _phase4(nc, R, H):
    with ExitStack() as es:
        sH2 = es.enter_context(nc.sbuf_tensor("sH2", [128, NU, 512], BF16)).ap()
        sXnTc = es.enter_context(nc.sbuf_tensor("sXnTc", [128, NDC, 512], BF16)).ap()
        sW13t = es.enter_context(nc.sbuf_tensor("sW13t", [128, 3, 2 * D], BF16)).ap()
        sW2t = es.enter_context(nc.sbuf_tensor("sW2t", [128, 4, 1024], BF16)).ap()
        sWoT = es.enter_context(nc.sbuf_tensor("sWoT", [128, 2, 1024], BF16)).ap()
        sAt = es.enter_context(nc.sbuf_tensor("sAt", [128, 2, 512], BF16)).ap()
        sSil = es.enter_context(nc.sbuf_tensor("sSil", [128, 2, 512], F32)).ap()
        sHb = es.enter_context(nc.sbuf_tensor("sHb", [128, 2, 512], BF16)).ap()
        sFM = es.enter_context(nc.sbuf_tensor("sFM", [128, NDC, 512], F32)).ap()
        sXo = es.enter_context(nc.sbuf_tensor("p4sXo", [128, 2, D], F32)).ap()
        sOutT = es.enter_context(nc.sbuf_tensor("sOutT", [128, 2, D], F32)).ap()

        xnl = R.sem("p4_xnl")
        wlg = [R.sem("p4_wl0"), R.sem("p4_wl1"), R.sem("p4_wl2")]
        w2tg = [R.sem(f"p4_w2t{i}") for i in range(4)]
        atg = [R.sem("p4_at0"), R.sem("p4_at1")]
        glg = [R.sem("p4_gl0"), R.sem("p4_gl1")]
        xlg = [R.sem("p4_xl0"), R.sem("p4_xl1")]
        oSg = [R.sem("p4_oS0"), R.sem("p4_oS1")]
        tU = R.sem("p4_tU")
        tL = R.sem("p4_tL")
        tBB = R.sem("p4_tB")
        tG = R.sem("p4_tG")
        tOT = R.sem("p4_tOT")
        aSi = R.sem("p4_aSi")
        aH2 = R.sem("p4_aH2")
        aG = R.sem("p4_aG")
        vMu = R.sem("p4_vMu")
        vO = R.sem("p4_vO")

        xo, wout = H["xo"], H["wout"]
        XNT, AS, W13S, W2S = H["XNT"], H["AS"], H["W13S"], H["W2S"]
        cH64, cIdF, out, CONST = H["cH64"], H["cIdF"], H["out"], H["CONST"]

        sW13_free = [None, None, None]
        sW2_free = [None, None, None, None]
        sWo_free = [None, None]
        sAt_free = [None, None]
        sSil_free = [None, None]
        sHb_free = [None, None]
        sXo_free = [None, None]
        sOutT_free = [None, None]
        sH2_guard = [None]
        sXnTc_guard = [None]
        sFM_guard = [None]
        bank_guard = [None]

        for c in range(4):
            R.dma("sync", sXnTc[:, :, :], XNT[c, :, :, :], xnl,
                  waits=(sXnTc_guard[0],))
            xnt_ready = (xnl, xnl.n)

            with ExitStack() as es_b:
                pUp = [es_b.enter_context(
                    nc.psum_tensor(f"p4pUp{c}{i}", [128, 512], F32)).ap()
                    for i in range(2)]
                pLin = [es_b.enter_context(
                    nc.psum_tensor(f"p4pLin{c}{i}", [128, 512], F32)).ap()
                    for i in range(2)]
                pB = [es_b.enter_context(
                    nc.psum_tensor(f"p4pB{c}{i}", [128, 512], F32)).ap()
                    for i in range(2)]
                pUp_free = [bank_guard[0], bank_guard[0]]
                pLin_free = [bank_guard[0], bank_guard[0]]
                pB_free = [bank_guard[0], bank_guard[0]]
                mul_done = {}

                def stage_b(u):
                    b = u % 2
                    R.op("tensor",
                         lambda e, b=b: e.matmul(
                             pB[b][:], cH64[:], sHb[:, b, :],
                             start=True, stop=True),
                         waits=(mul_done[u], CONST, pB_free[b], sH2_guard[0]),
                         inc=tBB)
                    sHb_free[b] = (tBB, tBB.n)
                    R.op("scalar",
                         lambda e, b=b, u=u: e.activation(
                             out=sH2[:, u, :], in_=pB[b][:], func=AF.Copy,
                             scale=0.125),
                         waits=((tBB, tBB.n),), inc=aH2)
                    pB_free[b] = (aH2, aH2.n)

                for u in range(NU):
                    b = u % 2
                    b3 = u % 3
                    R.dma("sync", sW13t[:, b3, :], W13S[u, :, :], wlg[b3],
                          waits=(sW13_free[b3],))
                    wld = (wlg[b3], wlg[b3].n)
                    for dc in range(NDC):
                        R.op("tensor",
                             lambda e, b=b, b3=b3, dc=dc: e.matmul(
                                 pUp[b][:], sW13t[:, b3, 128 * dc:128 * (dc + 1)],
                                 sXnTc[:, dc, :], start=(dc == 0),
                                 stop=(dc == NDC - 1)),
                             waits=(wld, xnt_ready, pUp_free[b]) if dc == 0 else (),
                             inc=tU if dc == NDC - 1 else None)
                    for dc in range(NDC):
                        R.op("tensor",
                             lambda e, b=b, b3=b3, dc=dc: e.matmul(
                                 pLin[b][:], sW13t[:, b3, D + 128 * dc:D + 128 * (dc + 1)],
                                 sXnTc[:, dc, :], start=(dc == 0),
                                 stop=(dc == NDC - 1)),
                             waits=(pLin_free[b],) if dc == 0 else (),
                             inc=tL if dc == NDC - 1 else None)
                    sW13_free[b3] = (tL, tL.n)
                    if u > 0:
                        stage_b(u - 1)
                    R.op("scalar",
                         lambda e, b=b: e.activation(
                             out=sSil[:, b, :], in_=pUp[b][:], func=AF.Silu),
                         waits=((tU, tU.n), sSil_free[b]), inc=aSi)
                    pUp_free[b] = (aSi, aSi.n)
                    R.op("vector",
                         lambda e, b=b: e.tensor_tensor(
                             out=sHb[:, b, :], in0=pLin[b][:],
                             in1=sSil[:, b, :], op=ALU.mult),
                         waits=((tL, tL.n), (aSi, aSi.n), sHb_free[b]), inc=vMu)
                    pLin_free[b] = (vMu, vMu.n)
                    sSil_free[b] = (vMu, vMu.n)
                    mul_done[u] = (vMu, vMu.n)
                stage_b(NU - 1)
                h2_ready = (aH2, aH2.n)
                sXnTc_guard[0] = (tL, tL.n)
                R.flush()

            with ExitStack() as es_c:
                pG = es_c.enter_context(
                    nc.psum_tensor(f"p4pG{c}", [128, 8, 512], F32)).ap()
                pG_free = (aH2, aH2.n)
                for half in range(2):
                    for u in range(NU):
                        b = (half * NU + u) % 4
                        R.dma("scalar", sW2t[:, b, :],
                              W2S[u, :, 1024 * half:1024 * (half + 1)], w2tg[b],
                              waits=(sW2_free[b],))
                        wld = (w2tg[b], w2tg[b].n)
                        for d8 in range(8):
                            w = ()
                            if u == 0:
                                w = (wld, h2_ready, sFM_guard[0], pG_free)
                            elif d8 == 0:
                                w = (wld,)
                            R.op("tensor",
                                 lambda e, b=b, d8=d8, u=u: e.matmul(
                                     pG[:, d8, :],
                                     sW2t[:, b, 128 * d8:128 * (d8 + 1)],
                                     sH2[:, u, :], start=(u == 0), stop=False),
                                 waits=w, inc=tG if d8 == 7 else None)
                        sW2_free[b] = (tG, tG.n)
                    for cc in range(NDC):
                        b = cc % 2
                        R.dma("scalar", sWoT[:, b, :],
                              wout[128 * cc:128 * (cc + 1),
                                   1024 * half:1024 * (half + 1)],
                              glg[b], waits=(sWo_free[b], H["WOR"]))
                        R.dma("scalar", sAt[:, b, :],
                              AS[cc, :, 512 * c:512 * (c + 1)], atg[b],
                              waits=(sAt_free[b],))
                        wld2 = ((glg[b], glg[b].n), (atg[b], atg[b].n))
                        for d8 in range(8):
                            R.op("tensor",
                                 lambda e, b=b, d8=d8, cc=cc: e.matmul(
                                     pG[:, d8, :],
                                     sWoT[:, b, 128 * d8:128 * (d8 + 1)],
                                     sAt[:, b, :], start=False,
                                     stop=(cc == NDC - 1)),
                                 waits=wld2 if d8 == 0 else (),
                                 inc=tG if d8 == 7 else None)
                        sWo_free[b] = (tG, tG.n)
                        sAt_free[b] = (tG, tG.n)
                    grp_done = (tG, tG.n)
                    R.op("scalar",
                         lambda e, half=half: e.activation(
                             out=sFM[:, 8 * half:8 * (half + 1), :],
                             in_=pG[:, :, :], func=AF.Copy),
                         waits=(grp_done,), inc=aG)
                    pG_free = (aG, aG.n)
                sH2_guard[0] = (tG, tG.n)
                fm_ready = (aG, aG.n)
                R.flush()

            with ExitStack() as es_d:
                pOq = es_d.enter_context(
                    nc.psum_tensor(f"p4pOq{c}", [128, 4, 512], F32)).ap()
                pOq_free = [(aG, aG.n)] * 4
                for ts in range(4):
                    b = ts % 2
                    t = 4 * c + ts
                    R.dma("sync", sXo[:, b, :], xo[128 * t:128 * (t + 1), :],
                          xlg[b], waits=(sXo_free[b],))
                    xw = (xlg[b], xlg[b].n)
                    for dc in range(NDC):
                        qb = dc // 4
                        R.op("tensor",
                             lambda e, dc=dc, ts=ts, qb=qb: e.transpose(
                                 pOq[:, qb, 128 * (dc % 4):128 * (dc % 4 + 1)],
                                 sFM[:, dc, 128 * ts:128 * (ts + 1)], cIdF[:]),
                             waits=(fm_ready, CONST, pOq_free[qb])
                             if dc % 4 == 0 else (),
                             inc=tOT if dc % 4 == 3 else None)
                    tdone = (tOT, tOT.n)
                    for qb in range(4):
                        R.op("vector",
                             lambda e, qb=qb, b=b: e.tensor_tensor(
                                 out=sOutT[:, b, 512 * qb:512 * (qb + 1)],
                                 in0=pOq[:, qb, :],
                                 in1=sXo[:, b, 512 * qb:512 * (qb + 1)],
                                 op=ALU.add),
                             waits=(tdone, xw, sOutT_free[b]), inc=vO)
                        pOq_free[qb] = (vO, vO.n)
                    sXo_free[b] = (vO, vO.n)
                    R.dma("sync", out[128 * t:128 * (t + 1), :], sOutT[:, b, :],
                          oSg[b], waits=((vO, vO.n),))
                    sOutT_free[b] = oSg[b]
                sFM_guard[0] = (tOT, tOT.n)
                bank_guard[0] = (vO, vO.n)
                R.flush()

        return oSg


def kernel(L0_attn_x, L0_attn_w_qkv, L0_attn_w_out, L0_ffn_W1, L0_ffn_W2,
           L0_ffn_W3):
    x = np.asarray(L0_attn_x, dtype=np.float32)
    wqkv = np.asarray(L0_attn_w_qkv, dtype=np.float32)
    wq = np.ascontiguousarray(wqkv[:, 0:D])
    wk = np.ascontiguousarray(wqkv[:, D:2 * D])
    wv = np.ascontiguousarray(wqkv[:, 2 * D:3 * D])
    wout = np.ascontiguousarray(np.asarray(L0_attn_w_out, dtype=np.float32))
    w1 = np.ascontiguousarray(np.asarray(L0_ffn_W1, dtype=np.float32))
    w2 = np.ascontiguousarray(np.asarray(L0_ffn_W2, dtype=np.float32))
    w3 = np.ascontiguousarray(np.asarray(L0_ffn_W3, dtype=np.float32))

    h128c = _hadamard(128)
    h64c = _h64_i2()
    identf = np.eye(128, dtype=np.float32)
    masks = [np.ascontiguousarray(_amask(j)) for j in (0, 1)]

    nc = build_program()
    in_maps = []
    for c in range(8):
        b, j = c // 2, c % 2
        xb = np.ascontiguousarray(x[b])
        xov = np.ascontiguousarray(
            x[b].reshape(8, 512, D)[S_SETS[j]].reshape(OWN, D))
        in_maps.append(dict(xf=xb, xo=xov, wq=wq, wk=wk, wv=wv, wout=wout,
                            w1=w1, w3=w3, w2=w2, amask=masks[j], h128=h128c,
                            h64=h64c, identf=identf))
    res = run_bass_kernel_spmd(nc, in_maps, core_ids=list(range(8)))
    outa = np.empty((4, SEQ, D), dtype=np.float32)
    for c in range(8):
        b, j = c // 2, c % 2
        oc = res.results[c]["out"].reshape(4, 512, D)
        for l, g in enumerate(S_SETS[j]):
            outa[b, 512 * g:512 * (g + 1)] = oc[l]
    return outa


# version 15
# speedup vs baseline: 1.0190x; 1.0190x over previous
import numpy as np
import ml_dtypes
import concourse.bass as bass
import concourse.mybir as mybir
from concourse.bass_utils import run_bass_kernel_spmd
from contextlib import ExitStack

F32 = mybir.dt.float32
BF16 = mybir.dt.bfloat16
AF = mybir.ActivationFunctionType
ALU = mybir.AluOpType
AX = mybir.AxisListType

D = 2048
SEQ = 4096
OWN = 2048
NH = 16
DH = 128
FH = 8192
NDC = 16
NU = 64
SCALE_Q = 1.0 / float(np.sqrt(128.0))
S_SETS = ([0, 3, 4, 7], [1, 2, 5, 6])
KVS = [1024, 2048, 3072, 4096]
NEG = -1.0e30

ENGINES = ("tensor", "vector", "scalar", "gpsimd", "sync")


class Sem:
    __slots__ = ("h", "n", "name")

    def __init__(self, h, name):
        self.h = h
        self.n = 0
        self.name = name


class Rec:

    def __init__(self, nc, blk):
        self.nc = nc
        self.blk = blk
        self.ops = {e: [] for e in ENGINES}
        self.wm = {e: {} for e in ENGINES}

    def sem(self, name):
        return Sem(self.nc.alloc_semaphore(name), name)

    def op(self, eng, fn, waits=(), inc=None, inc_by=1):
        lst = self.ops[eng]
        for w in waits:
            if w is None:
                continue
            if isinstance(w, Sem):
                s, v = w, w.n
            else:
                s, v = w
            if v is None or v <= 0:
                continue
            if self.wm[eng].get(id(s), -1) < v:
                lst.append(("w", s, v))
                self.wm[eng][id(s)] = v
        if inc is not None:
            inc.n += inc_by
        if fn is not None:
            lst.append(("x", fn, inc, inc_by))
        return (inc, inc.n) if inc is not None else None

    def dma(self, q, out, in_, sem, waits=()):
        self.op(q, (lambda e, o=out, i=in_: e.dma_start(out=o, in_=i)),
                waits, inc=sem, inc_by=16)
        return (sem, sem.n)

    def dmaT(self, q, out, in_, sem, waits=()):
        self.op(q, (lambda e, o=out, i=in_: e.dma_start(out=o, in_=i,
                                                        transpose=True)),
                waits, inc=sem, inc_by=16)
        return (sem, sem.n)

    def fence(self, fences):
        for e in ENGINES:
            for (s, v) in fences:
                if v > 0 and self.wm[e].get(id(s), -1) < v:
                    self.ops[e].append(("w", s, v))
                    self.wm[e][id(s)] = v

    def flush(self):
        for e in ENGINES:
            lst = self.ops[e]
            if not lst:
                continue

            def body(eng, lst=tuple(lst)):
                for item in lst:
                    if item[0] == "w":
                        eng.wait_ge(item[1].h, item[2])
                    else:
                        _, fn, inc, k = item
                        ins = fn(eng)
                        if inc is not None:
                            ins.then_inc(inc.h, k)
            getattr(self.blk, e)(body)
            self.ops[e] = []


def _popcount_and(n):
    i = np.arange(n)
    pc = np.zeros((n, n), dtype=np.int64)
    a = i[:, None] & i[None, :]
    while np.any(a):
        pc += a & 1
        a >>= 1
    return pc


def _hadamard(n):
    return np.where(_popcount_and(n) % 2 == 0, 1.0, -1.0).astype(np.float32)


def _h64_i2():
    h64 = _hadamard(64)
    m = np.zeros((128, 128), dtype=np.float32)
    for r2 in range(2):
        m[r2::2, r2::2] = h64
    return m


def _amask(j):
    m = np.zeros((4, 4, 128, 1024), dtype=np.float32)
    for l in range(4):
        g = S_SETS[j][l]
        kvs = KVS[l]
        for i2 in range(4):
            qpos = g * 512 + i2 * 128 + np.arange(128)[:, None]
            jabs = kvs - 1024 + np.arange(1024)[None, :]
            m[l, i2] = np.where(jabs <= qpos, 0.0, NEG)
    return m.astype(ml_dtypes.bfloat16)


def _ln_tile(R, vS, aS, sX, sXsq, st, sXn, xld_wait, cEps, sq_guard=()):
    R.op("vector", lambda e, sX=sX, st=st: e.reduce_sum(st[:, 0:1], sX, axis=AX.X),
         waits=(xld_wait,), inc=vS)
    R.op("vector", lambda e, st=st: e.tensor_scalar_mul(st[:, 1:2], st[:, 0:1],
                                                        1.0 / D),
         waits=((vS, vS.n),), inc=vS)
    R.op("scalar", lambda e, sX=sX, st=st: e.activation(
        out=sXsq, in_=sX, func=AF.Square, accum_out=st[:, 2:3]),
        waits=(xld_wait,) + tuple(sq_guard), inc=aS)
    asq = (aS, aS.n)
    R.op("vector", lambda e, st=st: e.tensor_mul(st[:, 4:5], st[:, 1:2], st[:, 1:2]),
         waits=((vS, vS.n),), inc=vS)
    R.op("vector", lambda e, st=st: e.scalar_tensor_tensor(
        out=st[:, 3:4], in0=st[:, 2:3], scalar=1.0 / D, in1=st[:, 4:5],
        op0=ALU.mult, op1=ALU.subtract), waits=(asq, (vS, vS.n)), inc=vS)
    vvar = (vS, vS.n)
    R.op("scalar", lambda e, st=st, cEps=cEps: e.activation(
        out=st[:, 5:6], in_=st[:, 3:4], func=AF.Sqrt, bias=cEps[:, 0:1]),
        waits=(vvar,), inc=aS)
    asd = (aS, aS.n)
    R.op("vector", lambda e, st=st: e.reciprocal(st[:, 6:7], st[:, 5:6]),
         waits=(asd,), inc=vS)
    R.op("vector", lambda e, sX=sX, st=st, sXn=sXn: e.tensor_scalar(
        out=sXn, in0=sX, scalar1=st[:, 1:2], scalar2=st[:, 6:7],
        op0=ALU.subtract, op1=ALU.mult), waits=((vS, vS.n),), inc=vS)
    return (vS, vS.n)


def _xn_transposes(R, nc_t, sXn, sXnT, ts, pT, pT_free, tT, vS, cIdB,
                   xn_done, CONST, extra_guard, tkey):
    for k in range(4):
        pb = (tkey * 4 + k) % 2
        for q in range(4):
            dc = 4 * k + q
            w = (xn_done, CONST, pT_free[pb]) + tuple(extra_guard) if q == 0 else ()
            R.op("tensor",
                 lambda e, sXn=sXn, dc=dc, pb=pb, q=q: e.transpose(
                     pT[pb][:, 128 * q:128 * (q + 1)],
                     sXn[:, 128 * dc:128 * (dc + 1)], cIdB[:]),
                 waits=w, inc=tT if q == 3 else None)
        tdone = (tT, tT.n)
        R.op("vector",
             lambda e, k=k, ts=ts, pb=pb: e.tensor_copy(
                 out=sXnT[:, 4 * k:4 * (k + 1), 128 * ts:128 * (ts + 1)],
                 in_=pT[pb][:].rearrange("p (a b) -> p a b", a=4)),
             waits=(tdone,), inc=vS)
        pT_free[pb] = (vS, vS.n)


def build_program(debug=False):
    nc = bass.Bass("TRN2", target_bir_lowering=False, debug=False)

    xf = nc.dram_tensor("xf", [SEQ, D], F32, kind="ExternalInput").ap()
    xo = nc.dram_tensor("xo", [OWN, D], F32, kind="ExternalInput").ap()
    wq = nc.dram_tensor("wq", [D, D], F32, kind="ExternalInput").ap()
    wk = nc.dram_tensor("wk", [D, D], F32, kind="ExternalInput").ap()
    wv = nc.dram_tensor("wv", [D, D], F32, kind="ExternalInput").ap()
    wout = nc.dram_tensor("wout", [D, D], F32, kind="ExternalInput").ap()
    w1 = nc.dram_tensor("w1", [FH, D], F32, kind="ExternalInput").ap()
    w3 = nc.dram_tensor("w3", [FH, D], F32, kind="ExternalInput").ap()
    w2 = nc.dram_tensor("w2", [D, FH], F32, kind="ExternalInput").ap()
    amask = nc.dram_tensor("amask", [4, 4, 128, 1024], BF16, kind="ExternalInput").ap()
    h128 = nc.dram_tensor("h128", [128, 128], F32, kind="ExternalInput").ap()
    h64 = nc.dram_tensor("h64", [128, 128], F32, kind="ExternalInput").ap()
    identf = nc.dram_tensor("identf", [128, 128], F32, kind="ExternalInput").ap()
    out = nc.dram_tensor("out", [OWN, D], F32, kind="ExternalOutput").ap()

    WQB = nc.dram_tensor("WQB", [D, D], BF16, kind="Internal").ap()
    WKB = nc.dram_tensor("WKB", [D, D], BF16, kind="Internal").ap()
    WVB = nc.dram_tensor("WVB", [D, D], BF16, kind="Internal").ap()
    WOB = nc.dram_tensor("WOB", [D, D], BF16, kind="Internal").ap()
    KS = nc.dram_tensor("KS", [D, SEQ], BF16, kind="Internal").ap()
    VS = nc.dram_tensor("VS", [NH, 128, 32, DH], BF16, kind="Internal").ap()
    QS = nc.dram_tensor("QS", [4, 128, NH, 512], BF16, kind="Internal").ap()
    XNT = nc.dram_tensor("XNT", [4, 128, NDC, 512], BF16, kind="Internal").ap()
    AS = nc.dram_tensor("AS", [NH, 128, OWN], BF16, kind="Internal").ap()
    W13S = nc.dram_tensor("W13S", [NU, 128, 2 * D], BF16, kind="Internal").ap()
    W2S = nc.dram_tensor("W2S", [NU, 128, D], BF16, kind="Internal").ap()

    dbg = {}
    if debug:
        for name, t, shape in (("dKS", KS, [D, SEQ]), ("dVS", VS, [NH, 128, 32, DH]),
                               ("dQS", QS, [4, 128, NH, 512]),
                               ("dXNT", XNT, [4, 128, NDC, 512]),
                               ("dAS", AS, [NH, 128, OWN]),
                               ("dW13S", W13S, [NU, 128, 2 * D]),
                               ("dW2S", W2S, [NU, 128, D])):
            dbg[name] = (nc.dram_tensor(name, shape, BF16, kind="ExternalOutput").ap(), t)

    with ExitStack() as es_all:
        cIdB = es_all.enter_context(nc.sbuf_tensor("cIdB", [128, 128], BF16)).ap()
        cIdF = es_all.enter_context(nc.sbuf_tensor("cIdF", [128, 128], F32)).ap()
        cH128 = es_all.enter_context(nc.sbuf_tensor("cH128", [128, 128], BF16)).ap()
        cH64 = es_all.enter_context(nc.sbuf_tensor("cH64", [128, 128], BF16)).ap()
        cOnesF = es_all.enter_context(nc.sbuf_tensor("cOnesF", [1, 128], F32)).ap()
        cEps = es_all.enter_context(nc.sbuf_tensor("cEps", [128, 1], F32)).ap()

        blk = es_all.enter_context(nc.Block())
        R = Rec(nc, blk)

        cst = R.sem("cst")
        R.dma("gpsimd", cIdB[:], identf[:], cst)
        R.dma("sync", cIdF[:], identf[:], cst)
        R.dma("gpsimd", cH128[:], h128[:], cst)
        R.dma("gpsimd", cH64[:], h64[:], cst)
        CONST = (cst, cst.n)
        vcst = R.sem("vcst")
        R.op("vector", lambda e: e.memset(cOnesF[:], 1.0), inc=vcst)
        R.op("vector", lambda e: e.memset(cEps[:], 1.0e-5), inc=vcst)
        ONES = (vcst, vcst.n)

        WKR = R.dma("gpsimd", WKB[:, :], wk[:, :], R.sem("cwk"))
        WVR = R.dma("gpsimd", WVB[:, :], wv[:, :], R.sem("cwv"))
        WQR = R.dma("gpsimd", WQB[:, :], wq[:, :], R.sem("cwq"))
        WOR = R.dma("gpsimd", WOB[:, :], wout[:, :], R.sem("cwo"))

        H = dict(xf=xf, xo=xo, wq=WQB, wk=WKB, wv=WVB, wout=WOB, w1=w1,
                 w3=w3, w2=w2, amask=amask, out=out, KS=KS, VS=VS, QS=QS,
                 XNT=XNT, AS=AS, W13S=W13S, W2S=W2S, cIdB=cIdB, cIdF=cIdF,
                 cH128=cH128, cH64=cH64, cOnesF=cOnesF, cEps=cEps, CONST=CONST,
                 ONES=ONES, WKR=WKR, WVR=WVR, WQR=WQR, WOR=WOR)

        R.fence([CONST, ONES])
        f1 = _phase1(nc, R, H)
        R.fence(f1)
        f2 = _phase2(nc, R, H)
        R.fence(f2)
        f3 = _phase3(nc, R, H)
        R.fence(f3)
        oSg = _phase4(nc, R, H)

        if debug:
            dsem = R.sem("dbgs")
            for name, (o, t) in dbg.items():
                R.dma("sync", o[:], t[:], dsem)
            R.op("sync", None, waits=((dsem, dsem.n),))

        R.op("sync", None, waits=tuple((s, s.n) for s in oSg))
        R.flush()

    return nc


def _phase1(nc, R, H):
    with ExitStack() as es:
        sWk = es.enter_context(nc.sbuf_tensor("sWk", [128, NDC, D], BF16)).ap()
        sWv = es.enter_context(nc.sbuf_tensor("sWv", [128, NDC, D], BF16)).ap()
        sX = es.enter_context(nc.sbuf_tensor("sX", [128, 2, D], F32)).ap()
        sStat = es.enter_context(nc.sbuf_tensor("sStat", [128, 2, 8], F32)).ap()
        sXn = es.enter_context(nc.sbuf_tensor("sXn", [128, 2, D], BF16)).ap()
        sXnT = es.enter_context(nc.sbuf_tensor("sXnT", [128, NDC, 512], BF16)).ap()
        sKev = es.enter_context(nc.sbuf_tensor("sKev", [128, 2, 512], BF16)).ap()
        sVst = es.enter_context(nc.sbuf_tensor("sVst", [128, NH, 4, DH], BF16)).ap()
        sWld = es.enter_context(nc.sbuf_tensor("sWld", [128, 2, D], F32)).ap()
        sWst = es.enter_context(nc.sbuf_tensor("sWst", [128, 1, NDC * 128], BF16)).ap()
        pT = [es.enter_context(nc.psum_tensor(f"pT{i}", [128, 512], BF16)).ap()
              for i in range(2)]
        pK = [es.enter_context(nc.psum_tensor(f"pK{i}", [128, 512], F32)).ap()
              for i in range(2)]
        pV = [es.enter_context(nc.psum_tensor(f"pV{i}", [128, 512], F32)).ap()
              for i in range(2)]
        pWT = [es.enter_context(nc.psum_tensor(f"pWT{i}", [128, 512], F32)).ap()
               for i in range(2)]

        xldg = [R.sem("p1_xld0"), R.sem("p1_xld1")]
        wkvS = R.sem("p1_wkv")
        wldg = [R.sem("p1_wld0"), R.sem("p1_wld1")]
        vS = R.sem("p1_v")
        aS = R.sem("p1_a")
        tT = R.sem("p1_tT")
        tK = R.sem("p1_tK")
        tV = R.sem("p1_tV")
        tW = R.sem("p1_tW")
        kSg = [R.sem("p1_kS0"), R.sem("p1_kS1")]
        vsS = R.sem("p1_vsS")
        wSg = [R.sem("p1_wS0"), R.sem("p1_wS1")]

        xf, wk, wv, w1, w3 = H["xf"], H["wk"], H["wv"], H["w1"], H["w3"]
        KS, VS, W13S = H["KS"], H["VS"], H["W13S"]
        cIdB, CONST = H["cIdB"], H["CONST"]

        for w_src, w_dst, w_rdy in ((wk, sWk, H["WKR"]), (wv, sWv, H["WVR"])):
            for q in range(4):
                R.dma("scalar",
                      w_dst[:, 4 * q:4 * (q + 1), :],
                      w_src.rearrange("(c p) n -> p c n", p=128)[:, 4 * q:4 * (q + 1), :],
                      wkvS, waits=(w_rdy,))
        WKV = (wkvS, wkvS.n)

        sX_free = [None, None]
        pT_free = [None, None]
        pK_free = [None, None]
        pV_free = [None, None]
        sKev_free = [None, None]
        sWld_free = [None, None]
        sWst_free = [None, None]
        pWT_free = [None, None]
        sVst_free = [None]
        sXn_guard = [None, None]
        state = {"n_prep": 0}

        def prep_unit():
            pi = state["n_prep"]
            state["n_prep"] += 1
            if pi >= 128:
                return
            mat_src = w1 if pi < 64 else w3
            off = 0 if pi < 64 else D
            u = pi % 64
            b = pi % 2
            R.dma("scalar", sWld[:, b, :],
                  mat_src.rearrange("(a rr) d -> a rr d", rr=128)[:, 2 * u:2 * u + 2, :],
                  wldg[b], waits=(sWld_free[b],))
            ld = (wldg[b], wldg[b].n)
            for k in range(4):
                pb = (pi * 4 + k) % 2
                for q in range(4):
                    dc = 4 * k + q
                    R.op("tensor",
                         lambda e, b=b, dc=dc, pb=pb, q=q: e.transpose(
                             pWT[pb][:, 128 * q:128 * (q + 1)],
                             sWld[:, b, 128 * dc:128 * (dc + 1)], H["cIdF"][:]),
                         waits=(ld, CONST, pWT_free[pb]) if q == 0 else (),
                         inc=tW if q == 3 else None)
                tdone = (tW, tW.n)
                R.op("vector",
                     lambda e, k=k, pb=pb: e.tensor_copy(
                         out=sWst[:, 0, 512 * k:512 * (k + 1)], in_=pWT[pb][:]),
                     waits=(tdone, sWst_free[0]), inc=vS)
                pWT_free[pb] = (vS, vS.n)
            evdone = (vS, vS.n)
            sWld_free[b] = evdone
            R.dma("sync", W13S[u, :, off:off + D], sWst[:, 0, :], wSg[0],
                  waits=(evdone,))
            sWst_free[0] = wSg[0]

        for g in range(8):
            for ts in range(4):
                t = 4 * g + ts
                b = t % 2
                R.dma("sync", sX[:, b, :], xf[128 * t:128 * (t + 1), :],
                      xldg[b], waits=(sX_free[b],))
                xw = (xldg[b], xldg[b].n)
                xn_done = _ln_tile(R, vS, aS, sX[:, b, :], sXn[:, b, :],
                                   sStat[:, b, :], sXn[:, b, :], xw, H["cEps"],
                                   sq_guard=(sXn_guard[b],))
                sX_free[b] = xn_done
                _xn_transposes(R, nc, sXn[:, b, :], sXnT, ts, pT, pT_free,
                               tT, vS, cIdB, xn_done, CONST, (), t)
                sXn_guard[b] = (tT, tT.n)
                prep_unit()
                prep_unit()
            xnt_ready = (vS, vS.n)

            for kc in range(NDC):
                pb = kc % 2
                for dc in range(NDC):
                    R.op("tensor",
                         lambda e, kc=kc, dc=dc, pb=pb: e.matmul(
                             pK[pb][:], sWk[:, dc, 128 * kc:128 * (kc + 1)],
                             sXnT[:, dc, :], start=(dc == 0), stop=(dc == NDC - 1)),
                         waits=(xnt_ready, WKV, pK_free[pb]) if dc == 0 else (),
                         inc=tK if dc == NDC - 1 else None)
                mmdone = (tK, tK.n)
                eb = kc % 2
                R.op("scalar",
                     lambda e, eb=eb, pb=pb: e.activation(
                         out=sKev[:, eb, :], in_=pK[pb][:], func=AF.Copy),
                     waits=(mmdone, sKev_free[eb]), inc=aS)
                pK_free[pb] = (aS, aS.n)
                R.dma("sync", KS[128 * kc:128 * (kc + 1), 512 * g:512 * (g + 1)],
                      sKev[:, eb, :], kSg[eb], waits=((aS, aS.n),))
                sKev_free[eb] = kSg[eb]

            for ts in range(4):
                for vc in range(4):
                    pb = (ts * 4 + vc) % 2
                    for dc in range(NDC):
                        R.op("tensor",
                             lambda e, ts=ts, vc=vc, dc=dc, pb=pb: e.matmul(
                                 pV[pb][:], sXnT[:, dc, 128 * ts:128 * (ts + 1)],
                                 sWv[:, dc, 512 * vc:512 * (vc + 1)],
                                 start=(dc == 0), stop=(dc == NDC - 1)),
                             waits=(xnt_ready, WKV, pV_free[pb], sVst_free[0])
                             if dc == 0 else (),
                             inc=tV if dc == NDC - 1 else None)
                    mmdone = (tV, tV.n)
                    for hq in range(4):
                        h = 4 * vc + hq
                        R.op("vector",
                             lambda e, h=h, ts=ts, hq=hq, pb=pb: e.tensor_copy(
                                 out=sVst[:, h, ts, :],
                                 in_=pV[pb][:, 128 * hq:128 * (hq + 1)]),
                             waits=(mmdone,), inc=vS)
                    pV_free[pb] = (vS, vS.n)
            vst_done = (vS, vS.n)
            for h in range(NH):
                R.dma("sync", VS[h, :, 4 * g:4 * (g + 1), :], sVst[:, h, :, :],
                      vsS, waits=(vst_done,))
            sVst_free[0] = vsS

            prep_unit()
            prep_unit()
            prep_unit()
            prep_unit()

        while state["n_prep"] < 128:
            prep_unit()

        R.flush()
        sems = xldg + [wkvS] + wldg + [vS, aS, tT, tK, tV, tW] + kSg + \
            [vsS] + wSg
        return [(s, s.n) for s in sems]


def _phase2(nc, R, H):
    with ExitStack() as es:
        sWq = es.enter_context(nc.sbuf_tensor("sWq", [128, NDC, D], BF16)).ap()
        sX = es.enter_context(nc.sbuf_tensor("p2sX", [128, 2, D], F32)).ap()
        sXsq = es.enter_context(nc.sbuf_tensor("p2sXsq", [128, D], BF16)).ap()
        sStat = es.enter_context(nc.sbuf_tensor("p2sStat", [128, 2, 8], F32)).ap()
        sXn = es.enter_context(nc.sbuf_tensor("p2sXn", [128, 2, D], BF16)).ap()
        sXnT = es.enter_context(nc.sbuf_tensor("p2sXnT", [128, NDC, 512], BF16)).ap()
        sQev = es.enter_context(nc.sbuf_tensor("sQev", [128, 2, 512], BF16)).ap()
        sW2ld = es.enter_context(nc.sbuf_tensor("sW2ld", [128, 2, FH // 4], F32)).ap()
        sStg2 = es.enter_context(nc.sbuf_tensor("sStg2", [128, NU, 512], BF16)).ap()
        sW2ev = es.enter_context(nc.sbuf_tensor("sW2ev", [128, 2, 512], BF16)).ap()
        pT = [es.enter_context(nc.psum_tensor(f"p2pT{i}", [128, 512], BF16)).ap()
              for i in range(2)]
        pQ = [es.enter_context(nc.psum_tensor(f"p2pQ{i}", [128, 512], F32)).ap()
              for i in range(2)]
        pWT = [es.enter_context(nc.psum_tensor(f"p2pWT{i}", [128, 512], F32)).ap()
               for i in range(2)]
        pA = [es.enter_context(nc.psum_tensor(f"p2pA{i}", [128, 512], F32)).ap()
              for i in range(2)]

        xldg = [R.sem("p2_xld0"), R.sem("p2_xld1")]
        wqS = R.sem("p2_wq")
        w2ldg = [R.sem("p2_w2l0"), R.sem("p2_w2l1")]
        vS = R.sem("p2_v")
        aS = R.sem("p2_a")
        tT = R.sem("p2_tT")
        tQ = R.sem("p2_tQ")
        tW = R.sem("p2_tW")
        tA = R.sem("p2_tA")
        qSg = [R.sem("p2_qS0"), R.sem("p2_qS1")]
        xS = R.sem("p2_xS")
        w2Sg = [R.sem("p2_w2S0"), R.sem("p2_w2S1")]

        xo, wq, w2 = H["xo"], H["wq"], H["w2"]
        QS, XNT, W2S = H["QS"], H["XNT"], H["W2S"]
        cIdB, cH128, CONST = H["cIdB"], H["cH128"], H["CONST"]

        for q in range(4):
            R.dma("scalar", sWq[:, 4 * q:4 * (q + 1), :],
                  wq.rearrange("(c p) n -> p c n", p=128)[:, 4 * q:4 * (q + 1), :],
                  wqS, waits=(H["WQR"],))
        WQ = (wqS, wqS.n)

        sX_free = [None, None]
        pT_free = [None, None]
        pQ_free = [None, None]
        sQev_free = [None, None]
        pWT_free = [None, None]
        pA_free = [None, None]
        sW2ev_free = [None, None]
        sW2ld_free = [None, None]
        sXn_guard = [None, None]
        sXnT_guard = []
        sStg2_guard = []

        for g in range(4):
            for ts in range(4):
                t = 4 * g + ts
                b = t % 2
                R.dma("sync", sX[:, b, :], xo[128 * t:128 * (t + 1), :],
                      xldg[b], waits=(sX_free[b],))
                xw = (xldg[b], xldg[b].n)
                xn_done = _ln_tile(R, vS, aS, sX[:, b, :], sXn[:, b, :],
                                   sStat[:, b, :], sXn[:, b, :], xw, H["cEps"],
                                   sq_guard=(sXn_guard[b],))
                sX_free[b] = xn_done
                _xn_transposes(R, nc, sXn[:, b, :], sXnT, ts, pT, pT_free,
                               tT, vS, cIdB, xn_done, CONST, sXnT_guard, t)
                sXn_guard[b] = (tT, tT.n)
            xnt_ready = (vS, vS.n)
            R.dma("sync", XNT[g, :, :, :], sXnT[:, :, :], xS, waits=(xnt_ready,))
            xnt_stashed = (xS, xS.n)

            for qc in range(NDC):
                pb = qc % 2
                for dc in range(NDC):
                    R.op("tensor",
                         lambda e, qc=qc, dc=dc, pb=pb: e.matmul(
                             pQ[pb][:], sWq[:, dc, 128 * qc:128 * (qc + 1)],
                             sXnT[:, dc, :], start=(dc == 0), stop=(dc == NDC - 1)),
                         waits=(xnt_ready, WQ, pQ_free[pb]) if dc == 0 else (),
                         inc=tQ if dc == NDC - 1 else None)
                mmdone = (tQ, tQ.n)
                eb = qc % 2
                R.op("scalar",
                     lambda e, eb=eb, pb=pb: e.activation(
                         out=sQev[:, eb, :], in_=pQ[pb][:], func=AF.Copy,
                         scale=SCALE_Q),
                     waits=(mmdone, sQev_free[eb]), inc=aS)
                pQ_free[pb] = (aS, aS.n)
                R.dma("sync", QS[g, :, qc, :], sQev[:, eb, :], qSg[eb],
                      waits=((aS, aS.n),))
                sQev_free[eb] = qSg[eb]
            sXnT_guard = [(tQ, tQ.n), xS]

            for dq in range(4):
                dc = 4 * g + dq
                for hh in range(4):
                    lb = (dc * 4 + hh) % 2
                    R.dma("scalar", sW2ld[:, lb, :],
                          w2[128 * dc:128 * (dc + 1),
                             2048 * hh:2048 * (hh + 1)],
                          w2ldg[lb], waits=(sW2ld_free[lb],))
                    ld = (w2ldg[lb], w2ldg[lb].n)
                    for k in range(4):
                        ka = 4 * hh + k
                        pb = (dc * 16 + ka) % 2
                        for q in range(4):
                            a = 4 * k + q
                            w = (ld, CONST, pWT_free[pb]) + tuple(sStg2_guard) \
                                if q == 0 else ()
                            R.op("tensor",
                                 lambda e, a=a, pb=pb, q=q, lb=lb: e.transpose(
                                     pWT[pb][:, 128 * q:128 * (q + 1)],
                                     sW2ld[:, lb, 128 * a:128 * (a + 1)],
                                     H["cIdF"][:]),
                                 waits=w, inc=tW if q == 3 else None)
                        tdone = (tW, tW.n)
                        R.op("vector",
                             lambda e, ka=ka, dq=dq, pb=pb: e.tensor_copy(
                                 out=sStg2[:, 4 * ka:4 * (ka + 1),
                                           128 * dq:128 * (dq + 1)],
                                 in_=pWT[pb][:].rearrange("p (a b) -> p a b", a=4)),
                             waits=(tdone,), inc=vS)
                        pWT_free[pb] = (vS, vS.n)
                    sW2ld_free[lb] = (tW, tW.n)
            stg_ready = (vS, vS.n)
            for a in range(NU):
                pb = a % 2
                R.op("tensor",
                     lambda e, a=a, pb=pb: e.matmul(
                         pA[pb][:], cH128[:], sStg2[:, a, :],
                         start=True, stop=True),
                     waits=(stg_ready, CONST, pA_free[pb]), inc=tA)
                mmdone = (tA, tA.n)
                eb = a % 2
                R.op("scalar",
                     lambda e, eb=eb, pb=pb: e.activation(
                         out=sW2ev[:, eb, :], in_=pA[pb][:], func=AF.Copy,
                         scale=SCALE_Q),
                     waits=(mmdone, sW2ev_free[eb]), inc=aS)
                pA_free[pb] = (aS, aS.n)
                R.dma("sync", W2S[:, 2 * a:2 * a + 2, 512 * g:512 * (g + 1)],
                      sW2ev[:, eb, :], w2Sg[eb], waits=((aS, aS.n),))
                sW2ev_free[eb] = w2Sg[eb]
            sStg2_guard = [(tA, tA.n)]

        R.flush()
        sems = xldg + [wqS] + w2ldg + [vS, aS, tT, tQ, tW, tA, xS] + qSg + w2Sg
        return [(s, s.n) for s in sems]


def _phase3(nc, R, H):
    with ExitStack() as es:
        sAmL = es.enter_context(nc.sbuf_tensor("sAmL", [128, 2, 4, 1024], BF16)).ap()
        sS = es.enter_context(nc.sbuf_tensor("sS", [128, 4, 4096], F32)).ap()
        sP = es.enter_context(nc.sbuf_tensor("sP", [128, 2, 4, 4096], BF16)).ap()
        sPT = es.enter_context(nc.sbuf_tensor("sPT", [128, 4, 512], BF16)).ap()
        sK = es.enter_context(nc.sbuf_tensor("sK", [128, 2, 4096], BF16)).ap()
        sV = es.enter_context(nc.sbuf_tensor("sV", [128, 2, 32, DH], BF16)).ap()
        sQc = es.enter_context(nc.sbuf_tensor("sQc", [128, 2, 512], BF16)).ap()
        sCM = es.enter_context(nc.sbuf_tensor("sCM", [128, 4, 4], F32)).ap()
        sNM = es.enter_context(nc.sbuf_tensor("sNM", [128, 2, 4], F32)).ap()
        sL = es.enter_context(nc.sbuf_tensor("sL", [128, 2, 4], F32)).ap()
        sRi = es.enter_context(nc.sbuf_tensor("sRi", [128, 2, 4], F32)).ap()
        sA = es.enter_context(nc.sbuf_tensor("sA", [128, 2, 512], BF16)).ap()
        pS = [es.enter_context(nc.psum_tensor(f"p3pS{i}", [128, 2, 512], F32)).ap()
              for i in range(2)]
        pTT = es.enter_context(nc.psum_tensor("p3pTT", [128, 2, 512], BF16)).ap()
        pO = [es.enter_context(nc.psum_tensor(f"p3pO{i}", [128, 512], F32)).ap()
              for i in range(2)]

        dLg = [R.sem("p3_dL0"), R.sem("p3_dL1")]
        mldg = [R.sem("p3_mld0"), R.sem("p3_mld1")]
        dAg = [R.sem("p3_dA0"), R.sem("p3_dA1")]
        tQK = R.sem("p3_tQK")
        tTP = R.sem("p3_tTP")
        tPV = R.sem("p3_tPV")
        aSE = R.sem("p3_aSE")
        aPT = R.sem("p3_aPT")
        aE = R.sem("p3_aE")
        aA = R.sem("p3_aA")
        vSE = R.sem("p3_vSE")
        vSum = R.sem("p3_vSum")
        gP = R.sem("p3_gP")
        gM = R.sem("p3_gM")

        KS, VS, QS, AS, amask = H["KS"], H["VS"], H["QS"], H["AS"], H["amask"]
        CONST = H["CONST"]

        ld_free = [None, None]
        pS_free = [None, None]
        pTT_free = [None]
        pO_free = [None, None]
        sA_free = [None]
        sPT_base = [0]
        sNM_guard = [None, None]
        sL_guard = [None, None]
        sRi_guard = [None, None]
        amask_guard = [[], []]
        state = {}
        units = [(l, h) for l in range(4) for h in range(NH)]

        def deferred(p):
            l, h, kv, nb, n5, par, _exp_done, scale_done, _tTf = state[p]
            ev_jb = {}
            pv_base = [None]

            def emit_Tpair(jp):
                for sub in range(2):
                    jb = 2 * jp + sub
                    for i2 in range(4):
                        R.op("tensor",
                             lambda e, jb=jb, i2=i2, sub=sub, par=par: e.transpose(
                                 pTT[:, sub, 128 * i2:128 * (i2 + 1)],
                                 sP[:, par, i2, 128 * jb:128 * (jb + 1)],
                                 H["cIdB"][:]),
                             waits=(scale_done, CONST, pTT_free[0])
                             if (sub == 0 and i2 == 0) else (),
                             inc=tTP if (sub == 1 and i2 == 3) else None)
                tdone = (tTP, tTP.n)
                slot = (2 * jp) % 4
                w = [tdone]
                if pv_base[0] is None:
                    pv_base[0] = tPV.n
                if jp >= 2:
                    w.append((tPV, pv_base[0] + 2 * jp - 2))
                if jp % 2 == 0:
                    R.op("vector",
                         lambda e, jp=jp, slot=slot: e.tensor_copy(
                             out=sPT[:, slot:slot + 2, :], in_=pTT[:, :, :]),
                         waits=tuple(w), inc=vSE)
                    ev_jb[jp] = (vSE, vSE.n)
                else:
                    R.op("scalar",
                         lambda e, jp=jp, slot=slot: e.activation(
                             out=sPT[:, slot:slot + 2, :], in_=pTT[:, :, :],
                             func=AF.Identity),
                         waits=tuple(w), inc=aPT)
                    ev_jb[jp] = (aPT, aPT.n)
                pTT_free[0] = ev_jb[jp]

            def emit_PV(jb):
                w = [ev_jb[jb // 2]]
                if jb == 0:
                    w.append(pO_free[par])
                R.op("tensor",
                     lambda e, par=par, jb=jb, nb=nb: e.matmul(
                         pO[par][:], sV[:, par, jb, :], sPT[:, jb % 4, :],
                         start=(jb == 0), stop=(jb == nb - 1)),
                     waits=tuple(w), inc=tPV)

            steps = [lambda: emit_Tpair(0)]
            for jp in range(1, nb // 2):
                steps.append(lambda jp=jp: (emit_Tpair(jp), emit_PV(2 * jp - 2),
                                            emit_PV(2 * jp - 1)))
            steps.append(lambda: (emit_PV(nb - 2), emit_PV(nb - 1)))

            def tail():
                pv_done = (tPV, tPV.n)
                ld_free[par] = pv_done
                R.op("scalar",
                     lambda e, par=par: e.activation(
                         out=sA[:, 0, :], in_=pO[par][:], func=AF.Identity),
                     waits=(pv_done, sA_free[0]), inc=aA)
                pO_free[par] = (aA, aA.n)
                R.dma("scalar", AS[h, :, 512 * l:512 * (l + 1)], sA[:, 0, :],
                      dAg[par], waits=((aA, aA.n),))
                sA_free[0] = dAg[par]
                if h == NH - 1:
                    amask_guard[l % 2] = [(gM, gM.n)]
            return steps, tail

        loads_mark = {}
        masks_mark = {}

        def emit_loads(j):
            if j >= len(units):
                return
            lj, hj = units[j]
            kvj = KVS[lj]
            nbj = kvj // 128
            pj = j % 2
            msj = lj % 2
            if hj == 0:
                R.op("sync", None, waits=tuple(amask_guard[msj]))
                R.dma("sync", sAmL[:, msj, :, :],
                      amask.rearrange("l i p w -> l p i w")[lj], mldg[msj])
            masks_mark[j] = (mldg[msj], mldg[msj].n)
            R.dma("sync", sK[:, pj, 0:kvj],
                  KS[128 * hj:128 * (hj + 1), 0:kvj],
                  dLg[pj], waits=(ld_free[pj],))
            R.dma("sync", sV[:, pj, 0:nbj, :], VS[hj, :, 0:nbj, :], dLg[pj])
            R.dma("sync", sQc[:, pj, :], QS[lj, :, hj, :], dLg[pj])
            loads_mark[j] = (dLg[pj], dLg[pj].n)

        emit_loads(0)
        for idx, (l, h) in enumerate(units):
            kv = KVS[l]
            nb = kv // 128
            n5 = kv // 512
            par = idx % 2
            ms = l % 2
            mask_ready = masks_mark[idx]
            lds = loads_mark[idx]

            prev_exp_rows = state[idx - 1][6] if idx > 0 else [None] * 4
            pexp_guard = state[idx - 1][8] if idx > 1 else None
            row_started = [False] * 4
            exp_rows = [None] * 4
            bcnt = [0]
            nbat = n5 // 2

            def emit_qk_batch(i2, bi, idx=idx, l=l, kv=kv, n5=n5, nbat=nbat,
                              par=par, ms=ms, lds=lds, mask_ready=mask_ready,
                              prev_exp_rows=prev_exp_rows,
                              pexp_guard=pexp_guard):
                pp = bcnt[0] % 2
                for k in range(2):
                    j5 = 2 * bi + k
                    R.op("tensor",
                         lambda e, par=par, i2=i2, j5=j5, k=k, pp=pp: e.matmul(
                             pS[pp][:, k, :],
                             sQc[:, par, 128 * i2:128 * (i2 + 1)],
                             sK[:, par, 512 * j5:512 * (j5 + 1)],
                             start=True, stop=True),
                         waits=(lds, pS_free[pp]) if k == 0 else (),
                         inc=tQK if k == 1 else None)
                mmdone = (tQK, tQK.n)
                j0 = 2 * bi
                w = [mmdone]
                if not row_started[i2]:
                    w.append(prev_exp_rows[i2])
                    row_started[i2] = True
                if bcnt[0] % 4 == 3:
                    R.op("vector",
                         lambda e, i2=i2, j0=j0, pp=pp: e.tensor_copy(
                             out=sS[:, i2, 512 * j0:512 * (j0 + 2)],
                             in_=pS[pp][:, :, :]),
                         waits=tuple(w), inc=vSE)
                    ev = (vSE, vSE.n)
                else:
                    R.op("scalar",
                         lambda e, i2=i2, j0=j0, pp=pp: e.activation(
                             out=sS[:, i2, 512 * j0:512 * (j0 + 2)],
                             in_=pS[pp][:, :, :], func=AF.Identity),
                         waits=tuple(w), inc=aSE)
                    ev = (aSE, aSE.n)
                pS_free[pp] = ev
                if bi == nbat - 1:
                    R.op("gpsimd",
                         lambda e, i2=i2, kv=kv, ms=ms: e.tensor_tensor(
                             out=sS[:, i2, kv - 1024:kv],
                             in0=sS[:, i2, kv - 1024:kv],
                             in1=sAmL[:, ms, i2, :], op=ALU.add),
                         waits=(ev, mask_ready), inc=gM)
                    ev = (gM, gM.n)
                R.op("vector",
                     lambda e, i2=i2, bi=bi, j0=j0: e.reduce_max(
                         sCM[:, i2, bi:bi + 1],
                         sS[:, i2, 512 * j0:512 * (j0 + 2)], axis=AX.X),
                     waits=(ev,), inc=vSE)
                bcnt[0] += 1
                if bi == nbat - 1:
                    R.op("vector",
                         lambda e, i2=i2, nbat=nbat, par=par: e.reduce_max(
                             sNM[:, par, i2:i2 + 1], sCM[:, i2, 0:nbat],
                             axis=AX.X, negate=True),
                         waits=((vSE, vSE.n), sNM_guard[par]), inc=vSE)
                    nm = (vSE, vSE.n)
                    R.op("scalar",
                         lambda e, i2=i2, kv=kv, par=par: e.activation(
                             out=sP[:, par, i2, 0:kv], in_=sS[:, i2, 0:kv],
                             func=AF.Exp, bias=sNM[:, par, i2:i2 + 1],
                             accum_out=sL[:, par, i2:i2 + 1]),
                         waits=(nm, sL_guard[par], pexp_guard), inc=aE)
                    exp_rows[i2] = (aE, aE.n)
                    R.op("vector",
                         lambda e, i2=i2, par=par: e.reciprocal(
                             sRi[:, par, i2:i2 + 1], sL[:, par, i2:i2 + 1]),
                         waits=((aE, aE.n), sRi_guard[par]), inc=vSum)
                    R.op("vector",
                         lambda e, i2=i2, kv=kv, par=par: e.tensor_scalar_mul(
                             sP[:, par, i2, 0:kv], sP[:, par, i2, 0:kv],
                             sRi[:, par, i2:i2 + 1]),
                         waits=((vSum, vSum.n),), inc=gP)

            qk_steps = [(i2, bi) for i2 in range(4) for bi in range(nbat)]
            if idx > 0:
                pair_steps, def_tail = deferred(idx - 1)
            else:
                pair_steps, def_tail = [], None
            nq, nd = len(qk_steps), len(pair_steps)
            qi = di = 0
            while qi < nq or di < nd:
                take_q = 2 if qi == 0 else 1
                for _ in range(take_q):
                    if qi < nq:
                        emit_qk_batch(*qk_steps[qi])
                        qi += 1
                if nq - qi <= 0:
                    while di < nd:
                        pair_steps[di]()
                        di += 1
                else:
                    want = int(round((qi / nq) * nd))
                    while di < min(want, nd):
                        pair_steps[di]()
                        di += 1
            if def_tail is not None:
                def_tail()
            emit_loads(idx + 1)

            sNM_guard[par] = (aE, aE.n)
            sL_guard[par] = (vSum, vSum.n)
            sRi_guard[par] = (gP, gP.n)
            state[idx] = (l, h, kv, nb, n5, par, exp_rows, (gP, gP.n),
                          (tTP, tTP.n))
        last_steps, last_tail = deferred(len(units) - 1)
        for st_fn in last_steps:
            st_fn()
        last_tail()

        R.flush()
        sems = dLg + dAg + mldg + [tQK, tTP, tPV, aSE, aPT, aE, aA, vSE,
                                   vSum, gP, gM]
        return [(s, s.n) for s in sems]


def _phase4(nc, R, H):
    with ExitStack() as es:
        sH2 = es.enter_context(nc.sbuf_tensor("sH2", [128, NU, 512], BF16)).ap()
        sXnTc = es.enter_context(nc.sbuf_tensor("sXnTc", [128, NDC, 512], BF16)).ap()
        sW13t = es.enter_context(nc.sbuf_tensor("sW13t", [128, 3, 2 * D], BF16)).ap()
        sW2t = es.enter_context(nc.sbuf_tensor("sW2t", [128, 4, 1024], BF16)).ap()
        sWoT = es.enter_context(nc.sbuf_tensor("sWoT", [128, 2, 1024], BF16)).ap()
        sAt = es.enter_context(nc.sbuf_tensor("sAt", [128, 2, 512], BF16)).ap()
        sSil = es.enter_context(nc.sbuf_tensor("sSil", [128, 2, 512], F32)).ap()
        sHb = es.enter_context(nc.sbuf_tensor("sHb", [128, 2, 512], BF16)).ap()
        sFM = es.enter_context(nc.sbuf_tensor("sFM", [128, NDC, 512], F32)).ap()
        sXo = es.enter_context(nc.sbuf_tensor("p4sXo", [128, 2, D], F32)).ap()
        sOutT = es.enter_context(nc.sbuf_tensor("sOutT", [128, 2, D], F32)).ap()

        xnl = R.sem("p4_xnl")
        wlg = [R.sem("p4_wl0"), R.sem("p4_wl1"), R.sem("p4_wl2")]
        w2tg = [R.sem(f"p4_w2t{i}") for i in range(4)]
        atg = [R.sem("p4_at0"), R.sem("p4_at1")]
        glg = [R.sem("p4_gl0"), R.sem("p4_gl1")]
        xlg = [R.sem("p4_xl0"), R.sem("p4_xl1")]
        oSg = [R.sem("p4_oS0"), R.sem("p4_oS1")]
        tU = R.sem("p4_tU")
        tL = R.sem("p4_tL")
        tBB = R.sem("p4_tB")
        tG = R.sem("p4_tG")
        tOT = R.sem("p4_tOT")
        aSi = R.sem("p4_aSi")
        aH2 = R.sem("p4_aH2")
        aG = R.sem("p4_aG")
        vMu = R.sem("p4_vMu")
        vO = R.sem("p4_vO")

        xo, wout = H["xo"], H["wout"]
        XNT, AS, W13S, W2S = H["XNT"], H["AS"], H["W13S"], H["W2S"]
        cH64, cIdF, out, CONST = H["cH64"], H["cIdF"], H["out"], H["CONST"]

        sW13_free = [None, None, None]
        sW2_free = [None, None, None, None]
        sWo_free = [None, None]
        sAt_free = [None, None]
        sSil_free = [None, None]
        sHb_free = [None, None]
        sXo_free = [None, None]
        sOutT_free = [None, None]
        sH2_guard = [None]
        sXnTc_guard = [None]
        sFM_guard = [None]
        bank_guard = [None]

        for c in range(4):
            R.dma("sync", sXnTc[:, :, :], XNT[c, :, :, :], xnl,
                  waits=(sXnTc_guard[0],))
            xnt_ready = (xnl, xnl.n)

            with ExitStack() as es_b:
                pUp = [es_b.enter_context(
                    nc.psum_tensor(f"p4pUp{c}{i}", [128, 512], F32)).ap()
                    for i in range(2)]
                pLin = [es_b.enter_context(
                    nc.psum_tensor(f"p4pLin{c}{i}", [128, 512], F32)).ap()
                    for i in range(2)]
                pB = [es_b.enter_context(
                    nc.psum_tensor(f"p4pB{c}{i}", [128, 512], F32)).ap()
                    for i in range(2)]
                pUp_free = [bank_guard[0], bank_guard[0]]
                pLin_free = [bank_guard[0], bank_guard[0]]
                pB_free = [bank_guard[0], bank_guard[0]]
                mul_done = {}

                def stage_b(u):
                    b = u % 2
                    R.op("tensor",
                         lambda e, b=b: e.matmul(
                             pB[b][:], cH64[:], sHb[:, b, :],
                             start=True, stop=True),
                         waits=(mul_done[u], CONST, pB_free[b], sH2_guard[0]),
                         inc=tBB)
                    sHb_free[b] = (tBB, tBB.n)
                    R.op("scalar",
                         lambda e, b=b, u=u: e.activation(
                             out=sH2[:, u, :], in_=pB[b][:], func=AF.Copy,
                             scale=0.125),
                         waits=((tBB, tBB.n),), inc=aH2)
                    pB_free[b] = (aH2, aH2.n)

                for u in range(NU):
                    b = u % 2
                    b3 = u % 3
                    R.dma("sync", sW13t[:, b3, :], W13S[u, :, :], wlg[b3],
                          waits=(sW13_free[b3],))
                    wld = (wlg[b3], wlg[b3].n)
                    for dc in range(NDC):
                        R.op("tensor",
                             lambda e, b=b, b3=b3, dc=dc: e.matmul(
                                 pUp[b][:], sW13t[:, b3, 128 * dc:128 * (dc + 1)],
                                 sXnTc[:, dc, :], start=(dc == 0),
                                 stop=(dc == NDC - 1)),
                             waits=(wld, xnt_ready, pUp_free[b]) if dc == 0 else (),
                             inc=tU if dc == NDC - 1 else None)
                    for dc in range(NDC):
                        R.op("tensor",
                             lambda e, b=b, b3=b3, dc=dc: e.matmul(
                                 pLin[b][:], sW13t[:, b3, D + 128 * dc:D + 128 * (dc + 1)],
                                 sXnTc[:, dc, :], start=(dc == 0),
                                 stop=(dc == NDC - 1)),
                             waits=(pLin_free[b],) if dc == 0 else (),
                             inc=tL if dc == NDC - 1 else None)
                    sW13_free[b3] = (tL, tL.n)
                    if u > 0:
                        stage_b(u - 1)
                    R.op("scalar",
                         lambda e, b=b: e.activation(
                             out=sSil[:, b, :], in_=pUp[b][:], func=AF.Silu),
                         waits=((tU, tU.n), sSil_free[b]), inc=aSi)
                    pUp_free[b] = (aSi, aSi.n)
                    R.op("vector",
                         lambda e, b=b: e.tensor_tensor(
                             out=sHb[:, b, :], in0=pLin[b][:],
                             in1=sSil[:, b, :], op=ALU.mult),
                         waits=((tL, tL.n), (aSi, aSi.n), sHb_free[b]), inc=vMu)
                    pLin_free[b] = (vMu, vMu.n)
                    sSil_free[b] = (vMu, vMu.n)
                    mul_done[u] = (vMu, vMu.n)
                stage_b(NU - 1)
                h2_ready = (aH2, aH2.n)
                sXnTc_guard[0] = (tL, tL.n)
                R.flush()

            with ExitStack() as es_c:
                pG = es_c.enter_context(
                    nc.psum_tensor(f"p4pG{c}", [128, 8, 512], F32)).ap()
                pG_free = (aH2, aH2.n)
                for half in range(2):
                    for u in range(NU):
                        b = (half * NU + u) % 4
                        R.dma("scalar", sW2t[:, b, :],
                              W2S[u, :, 1024 * half:1024 * (half + 1)], w2tg[b],
                              waits=(sW2_free[b],))
                        wld = (w2tg[b], w2tg[b].n)
                        for d8 in range(8):
                            w = ()
                            if u == 0:
                                w = (wld, h2_ready, sFM_guard[0], pG_free)
                            elif d8 == 0:
                                w = (wld,)
                            R.op("tensor",
                                 lambda e, b=b, d8=d8, u=u: e.matmul(
                                     pG[:, d8, :],
                                     sW2t[:, b, 128 * d8:128 * (d8 + 1)],
                                     sH2[:, u, :], start=(u == 0), stop=False),
                                 waits=w, inc=tG if d8 == 7 else None)
                        sW2_free[b] = (tG, tG.n)
                    for cc in range(NDC):
                        b = cc % 2
                        R.dma("scalar", sWoT[:, b, :],
                              wout[128 * cc:128 * (cc + 1),
                                   1024 * half:1024 * (half + 1)],
                              glg[b], waits=(sWo_free[b], H["WOR"]))
                        R.dma("scalar", sAt[:, b, :],
                              AS[cc, :, 512 * c:512 * (c + 1)], atg[b],
                              waits=(sAt_free[b],))
                        wld2 = ((glg[b], glg[b].n), (atg[b], atg[b].n))
                        for d8 in range(8):
                            R.op("tensor",
                                 lambda e, b=b, d8=d8, cc=cc: e.matmul(
                                     pG[:, d8, :],
                                     sWoT[:, b, 128 * d8:128 * (d8 + 1)],
                                     sAt[:, b, :], start=False,
                                     stop=(cc == NDC - 1)),
                                 waits=wld2 if d8 == 0 else (),
                                 inc=tG if d8 == 7 else None)
                        sWo_free[b] = (tG, tG.n)
                        sAt_free[b] = (tG, tG.n)
                    grp_done = (tG, tG.n)
                    R.op("scalar",
                         lambda e, half=half: e.activation(
                             out=sFM[:, 8 * half:8 * (half + 1), :],
                             in_=pG[:, :, :], func=AF.Copy),
                         waits=(grp_done,), inc=aG)
                    pG_free = (aG, aG.n)
                sH2_guard[0] = (tG, tG.n)
                fm_ready = (aG, aG.n)
                R.flush()

            with ExitStack() as es_d:
                pOq = es_d.enter_context(
                    nc.psum_tensor(f"p4pOq{c}", [128, 4, 512], F32)).ap()
                pOq_free = [(aG, aG.n)] * 4
                for ts in range(4):
                    b = ts % 2
                    t = 4 * c + ts
                    R.dma("sync", sXo[:, b, :], xo[128 * t:128 * (t + 1), :],
                          xlg[b], waits=(sXo_free[b],))
                    xw = (xlg[b], xlg[b].n)
                    for dc in range(NDC):
                        qb = dc // 4
                        R.op("tensor",
                             lambda e, dc=dc, ts=ts, qb=qb: e.transpose(
                                 pOq[:, qb, 128 * (dc % 4):128 * (dc % 4 + 1)],
                                 sFM[:, dc, 128 * ts:128 * (ts + 1)], cIdF[:]),
                             waits=(fm_ready, CONST, pOq_free[qb])
                             if dc % 4 == 0 else (),
                             inc=tOT if dc % 4 == 3 else None)
                    tdone = (tOT, tOT.n)
                    for qb in range(4):
                        R.op("vector",
                             lambda e, qb=qb, b=b: e.tensor_tensor(
                                 out=sOutT[:, b, 512 * qb:512 * (qb + 1)],
                                 in0=pOq[:, qb, :],
                                 in1=sXo[:, b, 512 * qb:512 * (qb + 1)],
                                 op=ALU.add),
                             waits=(tdone, xw, sOutT_free[b]), inc=vO)
                        pOq_free[qb] = (vO, vO.n)
                    sXo_free[b] = (vO, vO.n)
                    R.dma("sync", out[128 * t:128 * (t + 1), :], sOutT[:, b, :],
                          oSg[b], waits=((vO, vO.n),))
                    sOutT_free[b] = oSg[b]
                sFM_guard[0] = (tOT, tOT.n)
                bank_guard[0] = (vO, vO.n)
                R.flush()

        return oSg


def kernel(L0_attn_x, L0_attn_w_qkv, L0_attn_w_out, L0_ffn_W1, L0_ffn_W2,
           L0_ffn_W3):
    x = np.asarray(L0_attn_x, dtype=np.float32)
    wqkv = np.asarray(L0_attn_w_qkv, dtype=np.float32)
    wq = np.ascontiguousarray(wqkv[:, 0:D])
    wk = np.ascontiguousarray(wqkv[:, D:2 * D])
    wv = np.ascontiguousarray(wqkv[:, 2 * D:3 * D])
    wout = np.ascontiguousarray(np.asarray(L0_attn_w_out, dtype=np.float32))
    w1 = np.ascontiguousarray(np.asarray(L0_ffn_W1, dtype=np.float32))
    w2 = np.ascontiguousarray(np.asarray(L0_ffn_W2, dtype=np.float32))
    w3 = np.ascontiguousarray(np.asarray(L0_ffn_W3, dtype=np.float32))

    h128c = _hadamard(128)
    h64c = _h64_i2()
    identf = np.eye(128, dtype=np.float32)
    masks = [np.ascontiguousarray(_amask(j)) for j in (0, 1)]

    nc = build_program()
    in_maps = []
    for c in range(8):
        b, j = c // 2, c % 2
        xb = np.ascontiguousarray(x[b])
        xov = np.ascontiguousarray(
            x[b].reshape(8, 512, D)[S_SETS[j]].reshape(OWN, D))
        in_maps.append(dict(xf=xb, xo=xov, wq=wq, wk=wk, wv=wv, wout=wout,
                            w1=w1, w3=w3, w2=w2, amask=masks[j], h128=h128c,
                            h64=h64c, identf=identf))
    res = run_bass_kernel_spmd(nc, in_maps, core_ids=list(range(8)))
    outa = np.empty((4, SEQ, D), dtype=np.float32)
    for c in range(8):
        b, j = c // 2, c % 2
        oc = res.results[c]["out"].reshape(4, 512, D)
        for l, g in enumerate(S_SETS[j]):
            outa[b, 512 * g:512 * (g + 1)] = oc[l]
    return outa


# version 16
# speedup vs baseline: 1.0454x; 1.0259x over previous
import numpy as np
import ml_dtypes
import concourse.bass as bass
import concourse.mybir as mybir
from concourse.bass_utils import run_bass_kernel_spmd
from contextlib import ExitStack

F32 = mybir.dt.float32
BF16 = mybir.dt.bfloat16
AF = mybir.ActivationFunctionType
ALU = mybir.AluOpType
AX = mybir.AxisListType

D = 2048
SEQ = 4096
OWN = 2048
NH = 16
DH = 128
FH = 8192
NDC = 16
NU = 64
SCALE_Q = 1.0 / float(np.sqrt(128.0))
S_SETS = ([0, 3, 4, 7], [1, 2, 5, 6])
KVS = [1024, 2048, 3072, 4096]
NEG = -1.0e30

ENGINES = ("tensor", "vector", "scalar", "gpsimd", "sync")


class Sem:
    __slots__ = ("h", "n", "name")

    def __init__(self, h, name):
        self.h = h
        self.n = 0
        self.name = name


class Rec:

    def __init__(self, nc, blk):
        self.nc = nc
        self.blk = blk
        self.ops = {e: [] for e in ENGINES}
        self.wm = {e: {} for e in ENGINES}

    def sem(self, name):
        return Sem(self.nc.alloc_semaphore(name), name)

    def op(self, eng, fn, waits=(), inc=None, inc_by=1):
        lst = self.ops[eng]
        for w in waits:
            if w is None:
                continue
            if isinstance(w, Sem):
                s, v = w, w.n
            else:
                s, v = w
            if v is None or v <= 0:
                continue
            if self.wm[eng].get(id(s), -1) < v:
                lst.append(("w", s, v))
                self.wm[eng][id(s)] = v
        if inc is not None:
            inc.n += inc_by
        if fn is not None:
            lst.append(("x", fn, inc, inc_by))
        return (inc, inc.n) if inc is not None else None

    def dma(self, q, out, in_, sem, waits=()):
        self.op(q, (lambda e, o=out, i=in_: e.dma_start(out=o, in_=i)),
                waits, inc=sem, inc_by=16)
        return (sem, sem.n)

    def dmaT(self, q, out, in_, sem, waits=()):
        self.op(q, (lambda e, o=out, i=in_: e.dma_start(out=o, in_=i,
                                                        transpose=True)),
                waits, inc=sem, inc_by=16)
        return (sem, sem.n)

    def fence(self, fences):
        for e in ENGINES:
            for (s, v) in fences:
                if v > 0 and self.wm[e].get(id(s), -1) < v:
                    self.ops[e].append(("w", s, v))
                    self.wm[e][id(s)] = v

    def flush(self):
        for e in ENGINES:
            lst = self.ops[e]
            if not lst:
                continue

            def body(eng, lst=tuple(lst)):
                for item in lst:
                    if item[0] == "w":
                        eng.wait_ge(item[1].h, item[2])
                    else:
                        _, fn, inc, k = item
                        ins = fn(eng)
                        if inc is not None:
                            ins.then_inc(inc.h, k)
            getattr(self.blk, e)(body)
            self.ops[e] = []


def _popcount_and(n):
    i = np.arange(n)
    pc = np.zeros((n, n), dtype=np.int64)
    a = i[:, None] & i[None, :]
    while np.any(a):
        pc += a & 1
        a >>= 1
    return pc


def _hadamard(n):
    return np.where(_popcount_and(n) % 2 == 0, 1.0, -1.0).astype(np.float32)


def _h64_i2():
    h64 = _hadamard(64)
    m = np.zeros((128, 128), dtype=np.float32)
    for r2 in range(2):
        m[r2::2, r2::2] = h64
    return m


def _amask(j):
    m = np.zeros((4, 4, 128, 1024), dtype=np.float32)
    for l in range(4):
        g = S_SETS[j][l]
        kvs = KVS[l]
        for i2 in range(4):
            qpos = g * 512 + i2 * 128 + np.arange(128)[:, None]
            jabs = kvs - 1024 + np.arange(1024)[None, :]
            m[l, i2] = np.where(jabs <= qpos, 0.0, NEG)
    return m.astype(ml_dtypes.bfloat16)


def _ln_tile(R, vS, aS, sX, sXsq, st, sXn, xld_wait, cEps, sq_guard=()):
    R.op("vector", lambda e, sX=sX, st=st: e.reduce_sum(st[:, 0:1], sX, axis=AX.X),
         waits=(xld_wait,), inc=vS)
    R.op("vector", lambda e, st=st: e.tensor_scalar_mul(st[:, 1:2], st[:, 0:1],
                                                        1.0 / D),
         waits=((vS, vS.n),), inc=vS)
    R.op("scalar", lambda e, sX=sX, st=st: e.activation(
        out=sXsq, in_=sX, func=AF.Square, accum_out=st[:, 2:3]),
        waits=(xld_wait,) + tuple(sq_guard), inc=aS)
    asq = (aS, aS.n)
    R.op("vector", lambda e, st=st: e.tensor_mul(st[:, 4:5], st[:, 1:2], st[:, 1:2]),
         waits=((vS, vS.n),), inc=vS)
    R.op("vector", lambda e, st=st: e.scalar_tensor_tensor(
        out=st[:, 3:4], in0=st[:, 2:3], scalar=1.0 / D, in1=st[:, 4:5],
        op0=ALU.mult, op1=ALU.subtract), waits=(asq, (vS, vS.n)), inc=vS)
    vvar = (vS, vS.n)
    R.op("scalar", lambda e, st=st, cEps=cEps: e.activation(
        out=st[:, 5:6], in_=st[:, 3:4], func=AF.Sqrt, bias=cEps[:, 0:1]),
        waits=(vvar,), inc=aS)
    asd = (aS, aS.n)
    R.op("vector", lambda e, st=st: e.reciprocal(st[:, 6:7], st[:, 5:6]),
         waits=(asd,), inc=vS)
    R.op("vector", lambda e, sX=sX, st=st, sXn=sXn: e.tensor_scalar(
        out=sXn, in0=sX, scalar1=st[:, 1:2], scalar2=st[:, 6:7],
        op0=ALU.subtract, op1=ALU.mult), waits=((vS, vS.n),), inc=vS)
    return (vS, vS.n)


def _xn_transposes(R, nc_t, sXn, sXnT, ts, pT, pT_free, tT, vS, cIdB,
                   xn_done, CONST, extra_guard, tkey):
    for k in range(4):
        pb = (tkey * 4 + k) % 2
        for q in range(4):
            dc = 4 * k + q
            w = (xn_done, CONST, pT_free[pb]) + tuple(extra_guard) if q == 0 else ()
            R.op("tensor",
                 lambda e, sXn=sXn, dc=dc, pb=pb, q=q: e.transpose(
                     pT[pb][:, 128 * q:128 * (q + 1)],
                     sXn[:, 128 * dc:128 * (dc + 1)], cIdB[:]),
                 waits=w, inc=tT if q == 3 else None)
        tdone = (tT, tT.n)
        R.op("vector",
             lambda e, k=k, ts=ts, pb=pb: e.tensor_copy(
                 out=sXnT[:, 4 * k:4 * (k + 1), 128 * ts:128 * (ts + 1)],
                 in_=pT[pb][:].rearrange("p (a b) -> p a b", a=4)),
             waits=(tdone,), inc=vS)
        pT_free[pb] = (vS, vS.n)


def build_program(debug=False):
    nc = bass.Bass("TRN2", target_bir_lowering=False, debug=False)

    xf = nc.dram_tensor("xf", [SEQ, D], F32, kind="ExternalInput").ap()
    xo = nc.dram_tensor("xo", [OWN, D], F32, kind="ExternalInput").ap()
    wq = nc.dram_tensor("wq", [D, D], F32, kind="ExternalInput").ap()
    wk = nc.dram_tensor("wk", [D, D], F32, kind="ExternalInput").ap()
    wv = nc.dram_tensor("wv", [D, D], F32, kind="ExternalInput").ap()
    wout = nc.dram_tensor("wout", [D, D], F32, kind="ExternalInput").ap()
    w1 = nc.dram_tensor("w1", [FH, D], F32, kind="ExternalInput").ap()
    w3 = nc.dram_tensor("w3", [FH, D], F32, kind="ExternalInput").ap()
    w2 = nc.dram_tensor("w2", [D, FH], F32, kind="ExternalInput").ap()
    amask = nc.dram_tensor("amask", [4, 4, 128, 1024], BF16, kind="ExternalInput").ap()
    h128 = nc.dram_tensor("h128", [128, 128], F32, kind="ExternalInput").ap()
    h64 = nc.dram_tensor("h64", [128, 128], F32, kind="ExternalInput").ap()
    identf = nc.dram_tensor("identf", [128, 128], F32, kind="ExternalInput").ap()
    out = nc.dram_tensor("out", [OWN, D], F32, kind="ExternalOutput").ap()

    WQB = nc.dram_tensor("WQB", [D, D], BF16, kind="Internal").ap()
    WKB = nc.dram_tensor("WKB", [D, D], BF16, kind="Internal").ap()
    WVB = nc.dram_tensor("WVB", [D, D], BF16, kind="Internal").ap()
    WOB = nc.dram_tensor("WOB", [D, D], BF16, kind="Internal").ap()
    KS = nc.dram_tensor("KS", [D, SEQ], BF16, kind="Internal").ap()
    VS = nc.dram_tensor("VS", [NH, 128, 32, DH], BF16, kind="Internal").ap()
    QS = nc.dram_tensor("QS", [4, 128, NH, 512], BF16, kind="Internal").ap()
    XNT = nc.dram_tensor("XNT", [4, 128, NDC, 512], BF16, kind="Internal").ap()
    AS = nc.dram_tensor("AS", [NH, 128, OWN], BF16, kind="Internal").ap()
    W13S = nc.dram_tensor("W13S", [NU, 128, 2 * D], BF16, kind="Internal").ap()
    W2S = nc.dram_tensor("W2S", [NU, 128, D], BF16, kind="Internal").ap()

    dbg = {}
    if debug:
        for name, t, shape in (("dKS", KS, [D, SEQ]), ("dVS", VS, [NH, 128, 32, DH]),
                               ("dQS", QS, [4, 128, NH, 512]),
                               ("dXNT", XNT, [4, 128, NDC, 512]),
                               ("dAS", AS, [NH, 128, OWN]),
                               ("dW13S", W13S, [NU, 128, 2 * D]),
                               ("dW2S", W2S, [NU, 128, D])):
            dbg[name] = (nc.dram_tensor(name, shape, BF16, kind="ExternalOutput").ap(), t)

    with ExitStack() as es_all:
        cIdB = es_all.enter_context(nc.sbuf_tensor("cIdB", [128, 128], BF16)).ap()
        cIdF = es_all.enter_context(nc.sbuf_tensor("cIdF", [128, 128], F32)).ap()
        cH128 = es_all.enter_context(nc.sbuf_tensor("cH128", [128, 128], BF16)).ap()
        cH64 = es_all.enter_context(nc.sbuf_tensor("cH64", [128, 128], BF16)).ap()
        cOnesF = es_all.enter_context(nc.sbuf_tensor("cOnesF", [1, 128], F32)).ap()
        cEps = es_all.enter_context(nc.sbuf_tensor("cEps", [128, 1], F32)).ap()

        blk = es_all.enter_context(nc.Block())
        R = Rec(nc, blk)

        cst = R.sem("cst")
        R.dma("gpsimd", cIdB[:], identf[:], cst)
        R.dma("sync", cIdF[:], identf[:], cst)
        R.dma("gpsimd", cH128[:], h128[:], cst)
        R.dma("gpsimd", cH64[:], h64[:], cst)
        CONST = (cst, cst.n)
        vcst = R.sem("vcst")
        R.op("vector", lambda e: e.memset(cOnesF[:], 1.0), inc=vcst)
        R.op("vector", lambda e: e.memset(cEps[:], 1.0e-5), inc=vcst)
        ONES = (vcst, vcst.n)

        WKR = R.dma("gpsimd", WKB[:, :], wk[:, :], R.sem("cwk"))
        WVR = R.dma("gpsimd", WVB[:, :], wv[:, :], R.sem("cwv"))
        WQR = R.dma("gpsimd", WQB[:, :], wq[:, :], R.sem("cwq"))
        WOR = R.dma("gpsimd", WOB[:, :], wout[:, :], R.sem("cwo"))

        H = dict(xf=xf, xo=xo, wq=WQB, wk=WKB, wv=WVB, wout=WOB, w1=w1,
                 w3=w3, w2=w2, amask=amask, out=out, KS=KS, VS=VS, QS=QS,
                 XNT=XNT, AS=AS, W13S=W13S, W2S=W2S, cIdB=cIdB, cIdF=cIdF,
                 cH128=cH128, cH64=cH64, cOnesF=cOnesF, cEps=cEps, CONST=CONST,
                 ONES=ONES, WKR=WKR, WVR=WVR, WQR=WQR, WOR=WOR)

        R.fence([CONST, ONES])
        f1 = _phase1(nc, R, H)
        R.fence(f1)
        f2 = _phase2(nc, R, H)
        R.fence(f2)
        f3 = _phase3(nc, R, H)
        R.fence(f3)
        oSg = _phase4(nc, R, H)

        if debug:
            dsem = R.sem("dbgs")
            for name, (o, t) in dbg.items():
                R.dma("sync", o[:], t[:], dsem)
            R.op("sync", None, waits=((dsem, dsem.n),))

        R.op("sync", None, waits=tuple((s, s.n) for s in oSg))
        R.flush()

    return nc


def _phase1(nc, R, H):
    with ExitStack() as es:
        sWk = es.enter_context(nc.sbuf_tensor("sWk", [128, NDC, D], BF16)).ap()
        sWv = es.enter_context(nc.sbuf_tensor("sWv", [128, NDC, D], BF16)).ap()
        sX = es.enter_context(nc.sbuf_tensor("sX", [128, 2, D], F32)).ap()
        sStat = es.enter_context(nc.sbuf_tensor("sStat", [128, 2, 8], F32)).ap()
        sXn = es.enter_context(nc.sbuf_tensor("sXn", [128, 2, D], BF16)).ap()
        sXnT = es.enter_context(nc.sbuf_tensor("sXnT", [128, NDC, 512], BF16)).ap()
        sKev = es.enter_context(nc.sbuf_tensor("sKev", [128, 2, 512], BF16)).ap()
        sVst = es.enter_context(nc.sbuf_tensor("sVst", [128, NH, 4, DH], BF16)).ap()
        sWld = es.enter_context(nc.sbuf_tensor("sWld", [128, 2, D], F32)).ap()
        sWst = es.enter_context(nc.sbuf_tensor("sWst", [128, 1, NDC * 128], BF16)).ap()
        pT = [es.enter_context(nc.psum_tensor(f"pT{i}", [128, 512], BF16)).ap()
              for i in range(2)]
        pK = [es.enter_context(nc.psum_tensor(f"pK{i}", [128, 512], F32)).ap()
              for i in range(2)]
        pV = [es.enter_context(nc.psum_tensor(f"pV{i}", [128, 512], F32)).ap()
              for i in range(2)]
        pWT = [es.enter_context(nc.psum_tensor(f"pWT{i}", [128, 512], F32)).ap()
               for i in range(2)]

        xldg = [R.sem("p1_xld0"), R.sem("p1_xld1")]
        wkvS = R.sem("p1_wkv")
        wldg = [R.sem("p1_wld0"), R.sem("p1_wld1")]
        vS = R.sem("p1_v")
        aS = R.sem("p1_a")
        tT = R.sem("p1_tT")
        tK = R.sem("p1_tK")
        tV = R.sem("p1_tV")
        tW = R.sem("p1_tW")
        kSg = [R.sem("p1_kS0"), R.sem("p1_kS1")]
        vsS = R.sem("p1_vsS")
        wSg = [R.sem("p1_wS0"), R.sem("p1_wS1")]

        xf, wk, wv, w1, w3 = H["xf"], H["wk"], H["wv"], H["w1"], H["w3"]
        KS, VS, W13S = H["KS"], H["VS"], H["W13S"]
        cIdB, CONST = H["cIdB"], H["CONST"]

        for w_src, w_dst, w_rdy in ((wk, sWk, H["WKR"]), (wv, sWv, H["WVR"])):
            for q in range(4):
                R.dma("scalar",
                      w_dst[:, 4 * q:4 * (q + 1), :],
                      w_src.rearrange("(c p) n -> p c n", p=128)[:, 4 * q:4 * (q + 1), :],
                      wkvS, waits=(w_rdy,))
        WKV = (wkvS, wkvS.n)

        sX_free = [None, None]
        pT_free = [None, None]
        pK_free = [None, None]
        pV_free = [None, None]
        sKev_free = [None, None]
        sWld_free = [None, None]
        sWst_free = [None, None]
        pWT_free = [None, None]
        sVst_free = [None]
        sXn_guard = [None, None]
        state = {"n_prep": 0}

        def prep_unit():
            pi = state["n_prep"]
            state["n_prep"] += 1
            if pi >= 128:
                return
            mat_src = w1 if pi < 64 else w3
            off = 0 if pi < 64 else D
            u = pi % 64
            b = pi % 2
            R.dma("scalar", sWld[:, b, :],
                  mat_src.rearrange("(a rr) d -> a rr d", rr=128)[:, 2 * u:2 * u + 2, :],
                  wldg[b], waits=(sWld_free[b],))
            ld = (wldg[b], wldg[b].n)
            for k in range(4):
                pb = (pi * 4 + k) % 2
                for q in range(4):
                    dc = 4 * k + q
                    R.op("tensor",
                         lambda e, b=b, dc=dc, pb=pb, q=q: e.transpose(
                             pWT[pb][:, 128 * q:128 * (q + 1)],
                             sWld[:, b, 128 * dc:128 * (dc + 1)], H["cIdF"][:]),
                         waits=(ld, CONST, pWT_free[pb]) if q == 0 else (),
                         inc=tW if q == 3 else None)
                tdone = (tW, tW.n)
                R.op("vector",
                     lambda e, k=k, pb=pb: e.tensor_copy(
                         out=sWst[:, 0, 512 * k:512 * (k + 1)], in_=pWT[pb][:]),
                     waits=(tdone, sWst_free[0]), inc=vS)
                pWT_free[pb] = (vS, vS.n)
            evdone = (vS, vS.n)
            sWld_free[b] = evdone
            R.dma("sync", W13S[u, :, off:off + D], sWst[:, 0, :], wSg[0],
                  waits=(evdone,))
            sWst_free[0] = wSg[0]

        for g in range(8):
            for ts in range(4):
                t = 4 * g + ts
                b = t % 2
                R.dma("sync", sX[:, b, :], xf[128 * t:128 * (t + 1), :],
                      xldg[b], waits=(sX_free[b],))
                xw = (xldg[b], xldg[b].n)
                xn_done = _ln_tile(R, vS, aS, sX[:, b, :], sXn[:, b, :],
                                   sStat[:, b, :], sXn[:, b, :], xw, H["cEps"],
                                   sq_guard=(sXn_guard[b],))
                sX_free[b] = xn_done
                _xn_transposes(R, nc, sXn[:, b, :], sXnT, ts, pT, pT_free,
                               tT, vS, cIdB, xn_done, CONST, (), t)
                sXn_guard[b] = (tT, tT.n)
                prep_unit()
                prep_unit()
            xnt_ready = (vS, vS.n)

            for kc in range(NDC):
                pb = kc % 2
                for dc in range(NDC):
                    R.op("tensor",
                         lambda e, kc=kc, dc=dc, pb=pb: e.matmul(
                             pK[pb][:], sWk[:, dc, 128 * kc:128 * (kc + 1)],
                             sXnT[:, dc, :], start=(dc == 0), stop=(dc == NDC - 1)),
                         waits=(xnt_ready, WKV, pK_free[pb]) if dc == 0 else (),
                         inc=tK if dc == NDC - 1 else None)
                mmdone = (tK, tK.n)
                eb = kc % 2
                R.op("scalar",
                     lambda e, eb=eb, pb=pb: e.activation(
                         out=sKev[:, eb, :], in_=pK[pb][:], func=AF.Copy),
                     waits=(mmdone, sKev_free[eb]), inc=aS)
                pK_free[pb] = (aS, aS.n)
                R.dma("sync", KS[128 * kc:128 * (kc + 1), 512 * g:512 * (g + 1)],
                      sKev[:, eb, :], kSg[eb], waits=((aS, aS.n),))
                sKev_free[eb] = kSg[eb]

            for ts in range(4):
                for vc in range(4):
                    pb = (ts * 4 + vc) % 2
                    for dc in range(NDC):
                        R.op("tensor",
                             lambda e, ts=ts, vc=vc, dc=dc, pb=pb: e.matmul(
                                 pV[pb][:], sXnT[:, dc, 128 * ts:128 * (ts + 1)],
                                 sWv[:, dc, 512 * vc:512 * (vc + 1)],
                                 start=(dc == 0), stop=(dc == NDC - 1)),
                             waits=(xnt_ready, WKV, pV_free[pb], sVst_free[0])
                             if dc == 0 else (),
                             inc=tV if dc == NDC - 1 else None)
                    mmdone = (tV, tV.n)
                    for hq in range(4):
                        h = 4 * vc + hq
                        R.op("vector",
                             lambda e, h=h, ts=ts, hq=hq, pb=pb: e.tensor_copy(
                                 out=sVst[:, h, ts, :],
                                 in_=pV[pb][:, 128 * hq:128 * (hq + 1)]),
                             waits=(mmdone,), inc=vS)
                    pV_free[pb] = (vS, vS.n)
            vst_done = (vS, vS.n)
            for h in range(NH):
                R.dma("sync", VS[h, :, 4 * g:4 * (g + 1), :], sVst[:, h, :, :],
                      vsS, waits=(vst_done,))
            sVst_free[0] = vsS

            prep_unit()
            prep_unit()
            prep_unit()
            prep_unit()

        while state["n_prep"] < 128:
            prep_unit()

        R.flush()
        sems = xldg + [wkvS] + wldg + [vS, aS, tT, tK, tV, tW] + kSg + \
            [vsS] + wSg
        return [(s, s.n) for s in sems]


def _phase2(nc, R, H):
    with ExitStack() as es:
        sWq = es.enter_context(nc.sbuf_tensor("sWq", [128, NDC, D], BF16)).ap()
        sX = es.enter_context(nc.sbuf_tensor("p2sX", [128, 2, D], F32)).ap()
        sXsq = es.enter_context(nc.sbuf_tensor("p2sXsq", [128, D], BF16)).ap()
        sStat = es.enter_context(nc.sbuf_tensor("p2sStat", [128, 2, 8], F32)).ap()
        sXn = es.enter_context(nc.sbuf_tensor("p2sXn", [128, 2, D], BF16)).ap()
        sXnT = es.enter_context(nc.sbuf_tensor("p2sXnT", [128, NDC, 512], BF16)).ap()
        sQev = es.enter_context(nc.sbuf_tensor("sQev", [128, 2, 512], BF16)).ap()
        sW2ld = es.enter_context(nc.sbuf_tensor("sW2ld", [128, 2, FH // 4], F32)).ap()
        sStg2 = es.enter_context(nc.sbuf_tensor("sStg2", [128, NU, 512], BF16)).ap()
        sW2ev = es.enter_context(nc.sbuf_tensor("sW2ev", [128, 2, 512], BF16)).ap()
        pT = [es.enter_context(nc.psum_tensor(f"p2pT{i}", [128, 512], BF16)).ap()
              for i in range(2)]
        pQ = [es.enter_context(nc.psum_tensor(f"p2pQ{i}", [128, 512], F32)).ap()
              for i in range(2)]
        pWT = [es.enter_context(nc.psum_tensor(f"p2pWT{i}", [128, 512], F32)).ap()
               for i in range(2)]
        pA = [es.enter_context(nc.psum_tensor(f"p2pA{i}", [128, 512], F32)).ap()
              for i in range(2)]

        xldg = [R.sem("p2_xld0"), R.sem("p2_xld1")]
        wqS = R.sem("p2_wq")
        w2ldg = [R.sem("p2_w2l0"), R.sem("p2_w2l1")]
        vS = R.sem("p2_v")
        aS = R.sem("p2_a")
        tT = R.sem("p2_tT")
        tQ = R.sem("p2_tQ")
        tW = R.sem("p2_tW")
        tA = R.sem("p2_tA")
        qSg = [R.sem("p2_qS0"), R.sem("p2_qS1")]
        xS = R.sem("p2_xS")
        w2Sg = [R.sem("p2_w2S0"), R.sem("p2_w2S1")]

        xo, wq, w2 = H["xo"], H["wq"], H["w2"]
        QS, XNT, W2S = H["QS"], H["XNT"], H["W2S"]
        cIdB, cH128, CONST = H["cIdB"], H["cH128"], H["CONST"]

        for q in range(4):
            R.dma("scalar", sWq[:, 4 * q:4 * (q + 1), :],
                  wq.rearrange("(c p) n -> p c n", p=128)[:, 4 * q:4 * (q + 1), :],
                  wqS, waits=(H["WQR"],))
        WQ = (wqS, wqS.n)

        sX_free = [None, None]
        pT_free = [None, None]
        pQ_free = [None, None]
        sQev_free = [None, None]
        pWT_free = [None, None]
        pA_free = [None, None]
        sW2ev_free = [None, None]
        sW2ld_free = [None, None]
        sXn_guard = [None, None]
        sXnT_guard = []
        sStg2_guard = []

        for g in range(4):
            for ts in range(4):
                t = 4 * g + ts
                b = t % 2
                R.dma("sync", sX[:, b, :], xo[128 * t:128 * (t + 1), :],
                      xldg[b], waits=(sX_free[b],))
                xw = (xldg[b], xldg[b].n)
                xn_done = _ln_tile(R, vS, aS, sX[:, b, :], sXn[:, b, :],
                                   sStat[:, b, :], sXn[:, b, :], xw, H["cEps"],
                                   sq_guard=(sXn_guard[b],))
                sX_free[b] = xn_done
                _xn_transposes(R, nc, sXn[:, b, :], sXnT, ts, pT, pT_free,
                               tT, vS, cIdB, xn_done, CONST, sXnT_guard, t)
                sXn_guard[b] = (tT, tT.n)
            xnt_ready = (vS, vS.n)
            R.dma("sync", XNT[g, :, :, :], sXnT[:, :, :], xS, waits=(xnt_ready,))
            xnt_stashed = (xS, xS.n)

            for qc in range(NDC):
                pb = qc % 2
                for dc in range(NDC):
                    R.op("tensor",
                         lambda e, qc=qc, dc=dc, pb=pb: e.matmul(
                             pQ[pb][:], sWq[:, dc, 128 * qc:128 * (qc + 1)],
                             sXnT[:, dc, :], start=(dc == 0), stop=(dc == NDC - 1)),
                         waits=(xnt_ready, WQ, pQ_free[pb]) if dc == 0 else (),
                         inc=tQ if dc == NDC - 1 else None)
                mmdone = (tQ, tQ.n)
                eb = qc % 2
                R.op("scalar",
                     lambda e, eb=eb, pb=pb: e.activation(
                         out=sQev[:, eb, :], in_=pQ[pb][:], func=AF.Copy,
                         scale=SCALE_Q),
                     waits=(mmdone, sQev_free[eb]), inc=aS)
                pQ_free[pb] = (aS, aS.n)
                R.dma("sync", QS[g, :, qc, :], sQev[:, eb, :], qSg[eb],
                      waits=((aS, aS.n),))
                sQev_free[eb] = qSg[eb]
            sXnT_guard = [(tQ, tQ.n), xS]

            for dq in range(4):
                dc = 4 * g + dq
                for hh in range(4):
                    lb = (dc * 4 + hh) % 2
                    R.dma("scalar", sW2ld[:, lb, :],
                          w2[128 * dc:128 * (dc + 1),
                             2048 * hh:2048 * (hh + 1)],
                          w2ldg[lb], waits=(sW2ld_free[lb],))
                    ld = (w2ldg[lb], w2ldg[lb].n)
                    for k in range(4):
                        ka = 4 * hh + k
                        pb = (dc * 16 + ka) % 2
                        for q in range(4):
                            a = 4 * k + q
                            w = (ld, CONST, pWT_free[pb]) + tuple(sStg2_guard) \
                                if q == 0 else ()
                            R.op("tensor",
                                 lambda e, a=a, pb=pb, q=q, lb=lb: e.transpose(
                                     pWT[pb][:, 128 * q:128 * (q + 1)],
                                     sW2ld[:, lb, 128 * a:128 * (a + 1)],
                                     H["cIdF"][:]),
                                 waits=w, inc=tW if q == 3 else None)
                        tdone = (tW, tW.n)
                        R.op("vector",
                             lambda e, ka=ka, dq=dq, pb=pb: e.tensor_copy(
                                 out=sStg2[:, 4 * ka:4 * (ka + 1),
                                           128 * dq:128 * (dq + 1)],
                                 in_=pWT[pb][:].rearrange("p (a b) -> p a b", a=4)),
                             waits=(tdone,), inc=vS)
                        pWT_free[pb] = (vS, vS.n)
                    sW2ld_free[lb] = (tW, tW.n)
            stg_ready = (vS, vS.n)
            for a in range(NU):
                pb = a % 2
                R.op("tensor",
                     lambda e, a=a, pb=pb: e.matmul(
                         pA[pb][:], cH128[:], sStg2[:, a, :],
                         start=True, stop=True),
                     waits=(stg_ready, CONST, pA_free[pb]), inc=tA)
                mmdone = (tA, tA.n)
                eb = a % 2
                R.op("scalar",
                     lambda e, eb=eb, pb=pb: e.activation(
                         out=sW2ev[:, eb, :], in_=pA[pb][:], func=AF.Copy,
                         scale=SCALE_Q),
                     waits=(mmdone, sW2ev_free[eb]), inc=aS)
                pA_free[pb] = (aS, aS.n)
                R.dma("sync", W2S[:, 2 * a:2 * a + 2, 512 * g:512 * (g + 1)],
                      sW2ev[:, eb, :], w2Sg[eb], waits=((aS, aS.n),))
                sW2ev_free[eb] = w2Sg[eb]
            sStg2_guard = [(tA, tA.n)]

        R.flush()
        sems = xldg + [wqS] + w2ldg + [vS, aS, tT, tQ, tW, tA, xS] + qSg + w2Sg
        return [(s, s.n) for s in sems]


def _phase3(nc, R, H):
    with ExitStack() as es:
        sAmL = es.enter_context(nc.sbuf_tensor("sAmL", [128, 4, 4, 1024], BF16)).ap()
        sS = es.enter_context(nc.sbuf_tensor("sS", [128, 4, 4096], F32)).ap()
        sP = es.enter_context(nc.sbuf_tensor("sP", [128, 2, 4, 4096], BF16)).ap()
        sPT = es.enter_context(nc.sbuf_tensor("sPT", [128, 4, 512], BF16)).ap()
        sK = es.enter_context(nc.sbuf_tensor("sK", [128, 2, 4096], BF16)).ap()
        sV = es.enter_context(nc.sbuf_tensor("sV", [128, 2, 32, DH], BF16)).ap()
        sQc = es.enter_context(nc.sbuf_tensor("sQc", [128, 2, 512], BF16)).ap()
        sCM = es.enter_context(nc.sbuf_tensor("sCM", [128, 4, 4], F32)).ap()
        sNM = es.enter_context(nc.sbuf_tensor("sNM", [128, 2, 4], F32)).ap()
        sL = es.enter_context(nc.sbuf_tensor("sL", [128, 2, 4], F32)).ap()
        sRi = es.enter_context(nc.sbuf_tensor("sRi", [128, 2, 4], F32)).ap()
        sA = es.enter_context(nc.sbuf_tensor("sA", [128, 2, 512], BF16)).ap()
        pS = [es.enter_context(nc.psum_tensor(f"p3pS{i}", [128, 2, 512], F32)).ap()
              for i in range(2)]
        pTT = es.enter_context(nc.psum_tensor("p3pTT", [128, 2, 512], BF16)).ap()
        pO = [es.enter_context(nc.psum_tensor(f"p3pO{i}", [128, 512], F32)).ap()
              for i in range(2)]

        dLg = [R.sem("p3_dL0"), R.sem("p3_dL1")]
        mld = R.sem("p3_mld")
        dAg = [R.sem("p3_dA0"), R.sem("p3_dA1")]
        tQK = R.sem("p3_tQK")
        tTP = R.sem("p3_tTP")
        tPV = R.sem("p3_tPV")
        aSE = R.sem("p3_aSE")
        aPT = R.sem("p3_aPT")
        aE = R.sem("p3_aE")
        aA = R.sem("p3_aA")
        vSE = R.sem("p3_vSE")
        vSum = R.sem("p3_vSum")
        gP = R.sem("p3_gP")
        gM = R.sem("p3_gM")

        KS, VS, QS, AS, amask = H["KS"], H["VS"], H["QS"], H["AS"], H["amask"]
        CONST = H["CONST"]

        R.dma("sync", sAmL[:, :, :, :],
              amask.rearrange("l i p w -> p l i w"), mld)
        MASKS = (mld, mld.n)
        ld_free = [None, None]
        pS_free = [None, None]
        pTT_free = [None]
        pO_free = [None, None]
        sA_free = [None]
        sPT_base = [0]
        sNM_guard = [None, None]
        sL_guard = [None, None]
        sRi_guard = [None, None]
        state = {}
        units = [(l, h) for h in range(NH) for l in range(4)]

        def deferred(p):
            l, h, kv, nb, n5, par, _exp_done, scale_done, _tTf = state[p]
            ev_jb = {}
            pv_base = [None]

            def emit_Tpair(jp):
                for sub in range(2):
                    jb = 2 * jp + sub
                    for i2 in range(4):
                        R.op("tensor",
                             lambda e, jb=jb, i2=i2, sub=sub, par=par: e.transpose(
                                 pTT[:, sub, 128 * i2:128 * (i2 + 1)],
                                 sP[:, par, i2, 128 * jb:128 * (jb + 1)],
                                 H["cIdB"][:]),
                             waits=(scale_done, CONST, pTT_free[0])
                             if (sub == 0 and i2 == 0) else (),
                             inc=tTP if (sub == 1 and i2 == 3) else None)
                tdone = (tTP, tTP.n)
                slot = (2 * jp) % 4
                w = [tdone]
                if pv_base[0] is None:
                    pv_base[0] = tPV.n
                if jp >= 2:
                    w.append((tPV, pv_base[0] + 2 * jp - 2))
                if jp % 2 == 0:
                    R.op("vector",
                         lambda e, jp=jp, slot=slot: e.tensor_copy(
                             out=sPT[:, slot:slot + 2, :], in_=pTT[:, :, :]),
                         waits=tuple(w), inc=vSE)
                    ev_jb[jp] = (vSE, vSE.n)
                else:
                    R.op("scalar",
                         lambda e, jp=jp, slot=slot: e.activation(
                             out=sPT[:, slot:slot + 2, :], in_=pTT[:, :, :],
                             func=AF.Identity),
                         waits=tuple(w), inc=aPT)
                    ev_jb[jp] = (aPT, aPT.n)
                pTT_free[0] = ev_jb[jp]

            def emit_PV(jb):
                w = [ev_jb[jb // 2]]
                if jb == 0:
                    w.append(pO_free[par])
                R.op("tensor",
                     lambda e, par=par, jb=jb, nb=nb: e.matmul(
                         pO[par][:], sV[:, par, jb, :], sPT[:, jb % 4, :],
                         start=(jb == 0), stop=(jb == nb - 1)),
                     waits=tuple(w), inc=tPV)

            steps = [lambda: emit_Tpair(0)]
            for jp in range(1, nb // 2):
                steps.append(lambda jp=jp: (emit_Tpair(jp), emit_PV(2 * jp - 2),
                                            emit_PV(2 * jp - 1)))
            steps.append(lambda: (emit_PV(nb - 2), emit_PV(nb - 1)))

            def tail():
                pv_done = (tPV, tPV.n)
                ld_free[par] = pv_done
                R.op("scalar",
                     lambda e, par=par: e.activation(
                         out=sA[:, 0, :], in_=pO[par][:], func=AF.Identity),
                     waits=(pv_done, sA_free[0]), inc=aA)
                pO_free[par] = (aA, aA.n)
                R.dma("scalar", AS[h, :, 512 * l:512 * (l + 1)], sA[:, 0, :],
                      dAg[par], waits=((aA, aA.n),))
                sA_free[0] = dAg[par]
            return steps, tail

        loads_mark = {}

        def emit_loads(j):
            if j >= len(units):
                return
            lj, hj = units[j]
            kvj = KVS[lj]
            nbj = kvj // 128
            pj = j % 2
            R.dma("sync", sK[:, pj, 0:kvj],
                  KS[128 * hj:128 * (hj + 1), 0:kvj],
                  dLg[pj], waits=(ld_free[pj],))
            R.dma("sync", sV[:, pj, 0:nbj, :], VS[hj, :, 0:nbj, :], dLg[pj])
            R.dma("sync", sQc[:, pj, :], QS[lj, :, hj, :], dLg[pj])
            loads_mark[j] = (dLg[pj], dLg[pj].n)

        emit_loads(0)
        for idx, (l, h) in enumerate(units):
            kv = KVS[l]
            nb = kv // 128
            n5 = kv // 512
            par = idx % 2
            mask_ready = MASKS
            lds = loads_mark[idx]

            prev_exp_rows = state[idx - 1][6] if idx > 0 else [None] * 4
            pexp_guard = state[idx - 1][8] if idx > 1 else None
            row_started = [False] * 4
            exp_rows = [None] * 4
            bcnt = [0]
            nbat = n5 // 2

            def emit_qk_batch(i2, bi, idx=idx, l=l, kv=kv, n5=n5, nbat=nbat,
                              par=par, lds=lds, mask_ready=mask_ready,
                              prev_exp_rows=prev_exp_rows,
                              pexp_guard=pexp_guard):
                pp = bcnt[0] % 2
                for k in range(2):
                    j5 = 2 * bi + k
                    R.op("tensor",
                         lambda e, par=par, i2=i2, j5=j5, k=k, pp=pp: e.matmul(
                             pS[pp][:, k, :],
                             sQc[:, par, 128 * i2:128 * (i2 + 1)],
                             sK[:, par, 512 * j5:512 * (j5 + 1)],
                             start=True, stop=True),
                         waits=(lds, pS_free[pp]) if k == 0 else (),
                         inc=tQK if k == 1 else None)
                mmdone = (tQK, tQK.n)
                j0 = 2 * bi
                w = [mmdone]
                if not row_started[i2]:
                    w.append(prev_exp_rows[i2])
                    row_started[i2] = True
                if bi == nbat - 1:
                    w.append(mask_ready)
                    R.op("vector",
                         lambda e, i2=i2, j0=j0, pp=pp, l=l: e.tensor_tensor(
                             out=sS[:, i2, 512 * j0:512 * (j0 + 2)]
                             .rearrange("p (a b) -> p a b", a=2),
                             in0=pS[pp][:, :, :],
                             in1=sAmL[:, l, i2, :]
                             .rearrange("p (a b) -> p a b", a=2),
                             op=ALU.add),
                         waits=tuple(w), inc=vSE)
                    ev = (vSE, vSE.n)
                elif bcnt[0] % 4 == 3:
                    R.op("vector",
                         lambda e, i2=i2, j0=j0, pp=pp: e.tensor_copy(
                             out=sS[:, i2, 512 * j0:512 * (j0 + 2)],
                             in_=pS[pp][:, :, :]),
                         waits=tuple(w), inc=vSE)
                    ev = (vSE, vSE.n)
                else:
                    R.op("scalar",
                         lambda e, i2=i2, j0=j0, pp=pp: e.activation(
                             out=sS[:, i2, 512 * j0:512 * (j0 + 2)],
                             in_=pS[pp][:, :, :], func=AF.Identity),
                         waits=tuple(w), inc=aSE)
                    ev = (aSE, aSE.n)
                pS_free[pp] = ev
                R.op("vector",
                     lambda e, i2=i2, bi=bi, j0=j0: e.reduce_max(
                         sCM[:, i2, bi:bi + 1],
                         sS[:, i2, 512 * j0:512 * (j0 + 2)], axis=AX.X),
                     waits=(ev,), inc=vSE)
                bcnt[0] += 1
                if bi == nbat - 1:
                    R.op("vector",
                         lambda e, i2=i2, nbat=nbat, par=par: e.reduce_max(
                             sNM[:, par, i2:i2 + 1], sCM[:, i2, 0:nbat],
                             axis=AX.X, negate=True),
                         waits=((vSE, vSE.n), sNM_guard[par]), inc=vSE)
                    nm = (vSE, vSE.n)
                    R.op("scalar",
                         lambda e, i2=i2, kv=kv, par=par: e.activation(
                             out=sP[:, par, i2, 0:kv], in_=sS[:, i2, 0:kv],
                             func=AF.Exp, bias=sNM[:, par, i2:i2 + 1],
                             accum_out=sL[:, par, i2:i2 + 1]),
                         waits=(nm, sL_guard[par], pexp_guard), inc=aE)
                    exp_rows[i2] = (aE, aE.n)
                    R.op("vector",
                         lambda e, i2=i2, par=par: e.reciprocal(
                             sRi[:, par, i2:i2 + 1], sL[:, par, i2:i2 + 1]),
                         waits=((aE, aE.n), sRi_guard[par]), inc=vSum)
                    R.op("vector",
                         lambda e, i2=i2, kv=kv, par=par: e.tensor_scalar_mul(
                             sP[:, par, i2, 0:kv], sP[:, par, i2, 0:kv],
                             sRi[:, par, i2:i2 + 1]),
                         waits=((vSum, vSum.n),), inc=gP)

            qk_steps = [(i2, bi) for i2 in range(4) for bi in range(nbat)]
            if idx > 0:
                pair_steps, def_tail = deferred(idx - 1)
            else:
                pair_steps, def_tail = [], None
            nq, nd = len(qk_steps), len(pair_steps)
            qi = di = 0
            while qi < nq or di < nd:
                take_q = 2 if qi == 0 else 1
                for _ in range(take_q):
                    if qi < nq:
                        emit_qk_batch(*qk_steps[qi])
                        qi += 1
                if nq - qi <= 0:
                    while di < nd:
                        pair_steps[di]()
                        di += 1
                else:
                    want = int(round((qi / nq) * nd))
                    while di < min(want, nd):
                        pair_steps[di]()
                        di += 1
            if def_tail is not None:
                def_tail()
            emit_loads(idx + 1)

            sNM_guard[par] = (aE, aE.n)
            sL_guard[par] = (vSum, vSum.n)
            sRi_guard[par] = (gP, gP.n)
            state[idx] = (l, h, kv, nb, n5, par, exp_rows, (gP, gP.n),
                          (tTP, tTP.n))
        last_steps, last_tail = deferred(len(units) - 1)
        for st_fn in last_steps:
            st_fn()
        last_tail()

        R.flush()
        sems = dLg + dAg + [mld, tQK, tTP, tPV, aSE, aPT, aE, aA, vSE,
                            vSum, gP]
        return [(s, s.n) for s in sems]


def _phase4(nc, R, H):
    with ExitStack() as es:
        sH2 = es.enter_context(nc.sbuf_tensor("sH2", [128, NU, 512], BF16)).ap()
        sXnTc = es.enter_context(nc.sbuf_tensor("sXnTc", [128, NDC, 512], BF16)).ap()
        sW13t = es.enter_context(nc.sbuf_tensor("sW13t", [128, 3, 2 * D], BF16)).ap()
        sW2t = es.enter_context(nc.sbuf_tensor("sW2t", [128, 4, 1024], BF16)).ap()
        sWoT = es.enter_context(nc.sbuf_tensor("sWoT", [128, 2, 1024], BF16)).ap()
        sAt = es.enter_context(nc.sbuf_tensor("sAt", [128, 2, 512], BF16)).ap()
        sSil = es.enter_context(nc.sbuf_tensor("sSil", [128, 2, 512], F32)).ap()
        sHb = es.enter_context(nc.sbuf_tensor("sHb", [128, 2, 512], BF16)).ap()
        sFM = es.enter_context(nc.sbuf_tensor("sFM", [128, NDC, 512], F32)).ap()
        sXo = es.enter_context(nc.sbuf_tensor("p4sXo", [128, 2, D], F32)).ap()
        sOutT = es.enter_context(nc.sbuf_tensor("sOutT", [128, 2, D], F32)).ap()

        xnl = R.sem("p4_xnl")
        wlg = [R.sem("p4_wl0"), R.sem("p4_wl1"), R.sem("p4_wl2")]
        w2tg = [R.sem(f"p4_w2t{i}") for i in range(4)]
        atg = [R.sem("p4_at0"), R.sem("p4_at1")]
        glg = [R.sem("p4_gl0"), R.sem("p4_gl1")]
        xlg = [R.sem("p4_xl0"), R.sem("p4_xl1")]
        oSg = [R.sem("p4_oS0"), R.sem("p4_oS1")]
        tU = R.sem("p4_tU")
        tL = R.sem("p4_tL")
        tBB = R.sem("p4_tB")
        tG = R.sem("p4_tG")
        tOT = R.sem("p4_tOT")
        aSi = R.sem("p4_aSi")
        aH2 = R.sem("p4_aH2")
        aG = R.sem("p4_aG")
        vMu = R.sem("p4_vMu")
        vO = R.sem("p4_vO")

        xo, wout = H["xo"], H["wout"]
        XNT, AS, W13S, W2S = H["XNT"], H["AS"], H["W13S"], H["W2S"]
        cH64, cIdF, out, CONST = H["cH64"], H["cIdF"], H["out"], H["CONST"]

        sW13_free = [None, None, None]
        sW2_free = [None, None, None, None]
        sWo_free = [None, None]
        sAt_free = [None, None]
        sSil_free = [None, None]
        sHb_free = [None, None]
        sXo_free = [None, None]
        sOutT_free = [None, None]
        sH2_guard = [None]
        sXnTc_guard = [None]
        sFM_guard = [None]
        bank_guard = [None]

        for c in range(4):
            R.dma("sync", sXnTc[:, :, :], XNT[c, :, :, :], xnl,
                  waits=(sXnTc_guard[0],))
            xnt_ready = (xnl, xnl.n)

            with ExitStack() as es_b:
                pUp = [es_b.enter_context(
                    nc.psum_tensor(f"p4pUp{c}{i}", [128, 512], F32)).ap()
                    for i in range(2)]
                pLin = [es_b.enter_context(
                    nc.psum_tensor(f"p4pLin{c}{i}", [128, 512], F32)).ap()
                    for i in range(2)]
                pB = [es_b.enter_context(
                    nc.psum_tensor(f"p4pB{c}{i}", [128, 512], F32)).ap()
                    for i in range(2)]
                pUp_free = [bank_guard[0], bank_guard[0]]
                pLin_free = [bank_guard[0], bank_guard[0]]
                pB_free = [bank_guard[0], bank_guard[0]]
                mul_done = {}

                def stage_b(u):
                    b = u % 2
                    R.op("tensor",
                         lambda e, b=b: e.matmul(
                             pB[b][:], cH64[:], sHb[:, b, :],
                             start=True, stop=True),
                         waits=(mul_done[u], CONST, pB_free[b], sH2_guard[0]),
                         inc=tBB)
                    sHb_free[b] = (tBB, tBB.n)
                    R.op("scalar",
                         lambda e, b=b, u=u: e.activation(
                             out=sH2[:, u, :], in_=pB[b][:], func=AF.Copy,
                             scale=0.125),
                         waits=((tBB, tBB.n),), inc=aH2)
                    pB_free[b] = (aH2, aH2.n)

                for u in range(NU):
                    b = u % 2
                    b3 = u % 3
                    R.dma("sync", sW13t[:, b3, :], W13S[u, :, :], wlg[b3],
                          waits=(sW13_free[b3],))
                    wld = (wlg[b3], wlg[b3].n)
                    for dc in range(NDC):
                        R.op("tensor",
                             lambda e, b=b, b3=b3, dc=dc: e.matmul(
                                 pUp[b][:], sW13t[:, b3, 128 * dc:128 * (dc + 1)],
                                 sXnTc[:, dc, :], start=(dc == 0),
                                 stop=(dc == NDC - 1)),
                             waits=(wld, xnt_ready, pUp_free[b]) if dc == 0 else (),
                             inc=tU if dc == NDC - 1 else None)
                    for dc in range(NDC):
                        R.op("tensor",
                             lambda e, b=b, b3=b3, dc=dc: e.matmul(
                                 pLin[b][:], sW13t[:, b3, D + 128 * dc:D + 128 * (dc + 1)],
                                 sXnTc[:, dc, :], start=(dc == 0),
                                 stop=(dc == NDC - 1)),
                             waits=(pLin_free[b],) if dc == 0 else (),
                             inc=tL if dc == NDC - 1 else None)
                    sW13_free[b3] = (tL, tL.n)
                    if u > 0:
                        stage_b(u - 1)
                    R.op("scalar",
                         lambda e, b=b: e.activation(
                             out=sSil[:, b, :], in_=pUp[b][:], func=AF.Silu),
                         waits=((tU, tU.n), sSil_free[b]), inc=aSi)
                    pUp_free[b] = (aSi, aSi.n)
                    R.op("vector",
                         lambda e, b=b: e.tensor_tensor(
                             out=sHb[:, b, :], in0=pLin[b][:],
                             in1=sSil[:, b, :], op=ALU.mult),
                         waits=((tL, tL.n), (aSi, aSi.n), sHb_free[b]), inc=vMu)
                    pLin_free[b] = (vMu, vMu.n)
                    sSil_free[b] = (vMu, vMu.n)
                    mul_done[u] = (vMu, vMu.n)
                stage_b(NU - 1)
                h2_ready = (aH2, aH2.n)
                sXnTc_guard[0] = (tL, tL.n)
                R.flush()

            with ExitStack() as es_c:
                pG = es_c.enter_context(
                    nc.psum_tensor(f"p4pG{c}", [128, 8, 512], F32)).ap()
                pG_free = (aH2, aH2.n)
                for half in range(2):
                    for u in range(NU):
                        b = (half * NU + u) % 4
                        R.dma("scalar", sW2t[:, b, :],
                              W2S[u, :, 1024 * half:1024 * (half + 1)], w2tg[b],
                              waits=(sW2_free[b],))
                        wld = (w2tg[b], w2tg[b].n)
                        for d8 in range(8):
                            w = ()
                            if u == 0:
                                w = (wld, h2_ready, sFM_guard[0], pG_free)
                            elif d8 == 0:
                                w = (wld,)
                            R.op("tensor",
                                 lambda e, b=b, d8=d8, u=u: e.matmul(
                                     pG[:, d8, :],
                                     sW2t[:, b, 128 * d8:128 * (d8 + 1)],
                                     sH2[:, u, :], start=(u == 0), stop=False),
                                 waits=w, inc=tG if d8 == 7 else None)
                        sW2_free[b] = (tG, tG.n)
                    for cc in range(NDC):
                        b = cc % 2
                        R.dma("scalar", sWoT[:, b, :],
                              wout[128 * cc:128 * (cc + 1),
                                   1024 * half:1024 * (half + 1)],
                              glg[b], waits=(sWo_free[b], H["WOR"]))
                        R.dma("scalar", sAt[:, b, :],
                              AS[cc, :, 512 * c:512 * (c + 1)], atg[b],
                              waits=(sAt_free[b],))
                        wld2 = ((glg[b], glg[b].n), (atg[b], atg[b].n))
                        for d8 in range(8):
                            R.op("tensor",
                                 lambda e, b=b, d8=d8, cc=cc: e.matmul(
                                     pG[:, d8, :],
                                     sWoT[:, b, 128 * d8:128 * (d8 + 1)],
                                     sAt[:, b, :], start=False,
                                     stop=(cc == NDC - 1)),
                                 waits=wld2 if d8 == 0 else (),
                                 inc=tG if d8 == 7 else None)
                        sWo_free[b] = (tG, tG.n)
                        sAt_free[b] = (tG, tG.n)
                    grp_done = (tG, tG.n)
                    R.op("scalar",
                         lambda e, half=half: e.activation(
                             out=sFM[:, 8 * half:8 * (half + 1), :],
                             in_=pG[:, :, :], func=AF.Copy),
                         waits=(grp_done,), inc=aG)
                    pG_free = (aG, aG.n)
                sH2_guard[0] = (tG, tG.n)
                fm_ready = (aG, aG.n)
                R.flush()

            with ExitStack() as es_d:
                pOq = es_d.enter_context(
                    nc.psum_tensor(f"p4pOq{c}", [128, 4, 512], F32)).ap()
                pOq_free = [(aG, aG.n)] * 4
                for ts in range(4):
                    b = ts % 2
                    t = 4 * c + ts
                    R.dma("sync", sXo[:, b, :], xo[128 * t:128 * (t + 1), :],
                          xlg[b], waits=(sXo_free[b],))
                    xw = (xlg[b], xlg[b].n)
                    for dc in range(NDC):
                        qb = dc // 4
                        R.op("tensor",
                             lambda e, dc=dc, ts=ts, qb=qb: e.transpose(
                                 pOq[:, qb, 128 * (dc % 4):128 * (dc % 4 + 1)],
                                 sFM[:, dc, 128 * ts:128 * (ts + 1)], cIdF[:]),
                             waits=(fm_ready, CONST, pOq_free[qb])
                             if dc % 4 == 0 else (),
                             inc=tOT if dc % 4 == 3 else None)
                    tdone = (tOT, tOT.n)
                    for qb in range(4):
                        R.op("vector",
                             lambda e, qb=qb, b=b: e.tensor_tensor(
                                 out=sOutT[:, b, 512 * qb:512 * (qb + 1)],
                                 in0=pOq[:, qb, :],
                                 in1=sXo[:, b, 512 * qb:512 * (qb + 1)],
                                 op=ALU.add),
                             waits=(tdone, xw, sOutT_free[b]), inc=vO)
                        pOq_free[qb] = (vO, vO.n)
                    sXo_free[b] = (vO, vO.n)
                    R.dma("sync", out[128 * t:128 * (t + 1), :], sOutT[:, b, :],
                          oSg[b], waits=((vO, vO.n),))
                    sOutT_free[b] = oSg[b]
                sFM_guard[0] = (tOT, tOT.n)
                bank_guard[0] = (vO, vO.n)
                R.flush()

        return oSg


def kernel(L0_attn_x, L0_attn_w_qkv, L0_attn_w_out, L0_ffn_W1, L0_ffn_W2,
           L0_ffn_W3):
    x = np.asarray(L0_attn_x, dtype=np.float32)
    wqkv = np.asarray(L0_attn_w_qkv, dtype=np.float32)
    wq = np.ascontiguousarray(wqkv[:, 0:D])
    wk = np.ascontiguousarray(wqkv[:, D:2 * D])
    wv = np.ascontiguousarray(wqkv[:, 2 * D:3 * D])
    wout = np.ascontiguousarray(np.asarray(L0_attn_w_out, dtype=np.float32))
    w1 = np.ascontiguousarray(np.asarray(L0_ffn_W1, dtype=np.float32))
    w2 = np.ascontiguousarray(np.asarray(L0_ffn_W2, dtype=np.float32))
    w3 = np.ascontiguousarray(np.asarray(L0_ffn_W3, dtype=np.float32))

    h128c = _hadamard(128)
    h64c = _h64_i2()
    identf = np.eye(128, dtype=np.float32)
    masks = [np.ascontiguousarray(_amask(j)) for j in (0, 1)]

    nc = build_program()
    in_maps = []
    for c in range(8):
        b, j = c // 2, c % 2
        xb = np.ascontiguousarray(x[b])
        xov = np.ascontiguousarray(
            x[b].reshape(8, 512, D)[S_SETS[j]].reshape(OWN, D))
        in_maps.append(dict(xf=xb, xo=xov, wq=wq, wk=wk, wv=wv, wout=wout,
                            w1=w1, w3=w3, w2=w2, amask=masks[j], h128=h128c,
                            h64=h64c, identf=identf))
    res = run_bass_kernel_spmd(nc, in_maps, core_ids=list(range(8)))
    outa = np.empty((4, SEQ, D), dtype=np.float32)
    for c in range(8):
        b, j = c // 2, c % 2
        oc = res.results[c]["out"].reshape(4, 512, D)
        for l, g in enumerate(S_SETS[j]):
            outa[b, 512 * g:512 * (g + 1)] = oc[l]
    return outa
